# version 01
# baseline (speedup 1.0000x reference)
import contextlib
import numpy as np
import ml_dtypes
import concourse.bass as bass
import concourse.mybir as mybir
from concourse.bass_utils import run_bass_kernel_spmd

F32 = mybir.dt.float32
F32R = mybir.dt.float32r
BF16 = mybir.dt.bfloat16
AF = mybir.ActivationFunctionType
ALU = mybir.AluOpType
AX = mybir.AxisListType

D = 1024
DI = 2048
NS = 16
RK = 64
FFH = 8192
FFA = 4096
TH = 4096
L = 2048
NSEQ = 4
SLAB = 512
TC = 128
NCHUNK = SLAB // TC
NT = SLAB // 128
EPS = 1e-5
NCORE = 8

ENGINES = ("pe", "act", "dve", "pool", "sp")
MSEM = {"pe": "m_pe", "act": "m_act", "dve": "m_dve", "pool": "m_pool", "sp": "m_sp"}
DSEM = {"sp": "d_sp", "pool": "d_g"}
DTSIZE = {F32: 4, F32R: 4, BF16: 2}


def bc(ap, axis, n):
    pat = [list(x) for x in ap.ap]
    pat.insert(axis, [0, n])
    return bass.AP(ap.tensor, ap.offset, pat)


class Trk:
    __slots__ = ("w", "r")

    def __init__(self):
        self.w = None
        self.r = []

    def rd(self):
        return [self.w] if self.w else []

    def wr(self):
        return ([self.w] if self.w else []) + self.r

    def did_read(self, ev):
        self.r.append(ev)

    def did_write(self, ev):
        self.w = ev
        self.r = []


class Prog:
    def __init__(self):
        self.ops = {e: [] for e in ENGINES}
        self.cnt = {s: 0 for s in MSEM.values()}
        self.dma_sems = []

    def emit(self, eng, fn, deps=(), dma=False, dkey=None):
        if dma:
            sem = "d_" + (dkey or "misc")
            if sem not in self.cnt:
                self.cnt[sem] = 0
                self.dma_sems.append(sem)
            amt = 16
        else:
            sem = MSEM[eng]
            amt = 1
        self.cnt[sem] += amt
        ev = (sem, self.cnt[sem], amt)
        self.ops[eng].append((fn, [d for d in deps if d is not None], ev))
        return ev

    def waits(self, eng, deps):
        self.ops[eng].append((None, [d for d in deps if d is not None], None))

    def task(self, eng, fn, reads=(), writes=(), extra=(), dma=False, dkey=None):
        deps = list(extra)
        for b in reads:
            deps += b.rd()
        for b in writes:
            deps += b.wr()
        ev = self.emit(eng, fn, deps, dma=dma, dkey=dkey)
        for b in reads:
            b.did_read(ev)
        for b in writes:
            b.did_write(ev)
        return ev

    def __init_fences__(self):
        pass

    def replayer(self, sems):
        fences = getattr(self, "fences", {})
        def mk(name, engobj):
            waited = {}
            fence = fences.get(name)
            for fn, deps, inc in self.ops[name]:
                need = {}
                for (sem, val, _a) in deps:
                    if waited.get(sem, 0) >= val:
                        continue
                    need[sem] = max(need.get(sem, 0), val)
                for sem, val in sorted(need.items()):
                    engobj.wait_ge(sems[sem], val)
                    waited[sem] = val
                if fn is None:
                    continue
                instr = fn(engobj)
                if fence is not None:
                    instr = fence(engobj)
                sem, _val, amt = inc
                instr.then_inc(sems[sem], amt)
        return mk


class Scratch:

    def __init__(self, nc, name, nbytes, dtype=F32):
        self.cm = nc.sbuf_tensor("scr_" + name, [128, nbytes // 4], dtype)
        self.nbytes = nbytes
        self.off = 0
        self.events = []
        self.bar = []

    def __enter__(self):
        self.t = self.cm.__enter__()
        return self

    def __exit__(self, *a):
        return self.cm.__exit__(*a)

    def alloc(self, shape, dtype, parts=128):
        n = int(np.prod(shape)) * DTSIZE[dtype]
        n = (n + 15) // 16 * 16
        assert self.off + n <= self.nbytes, f"scratch overflow {self.off}+{n}"
        w0 = self.off // 4
        self.off += n
        flat = self.t[0:parts, w0:w0 + (n // 4)].bitcast(dtype)
        if len(shape) == 1:
            ap = flat[:, 0:shape[0]]
        elif len(shape) == 2:
            ap = flat[:, 0:shape[0] * shape[1]].rearrange(
                "p (a b) -> p a b", a=shape[0])
        else:
            ap = flat[:, 0:shape[0] * shape[1] * shape[2]].rearrange(
                "p (a b c) -> p a b c", a=shape[0], b=shape[1])
        return ap

    def reset(self):
        self.bar = self.bar + self.events
        self.events = []


def scr_task(p, regions, eng, fn, reads=(), writes=(), extra=(), dma=False,
             dkey=None):
    bar = []
    for r in regions:
        bar += r.bar
    ev = p.task(eng, fn, reads=reads, writes=writes,
                extra=list(extra) + bar, dma=dma, dkey=dkey)
    for r in regions:
        r.events.append(ev)
    return ev



def _tile_k_o(w, kblk, oblk, dtype=None):
    K, O = w.shape
    t = w.reshape(K // kblk, kblk, O // oblk, oblk).transpose(2, 0, 1, 3)
    if dtype is not None:
        t = t.astype(dtype)
    return np.ascontiguousarray(t)


def _perpart(v, nsl):
    return np.ascontiguousarray(v.reshape(nsl, 128).T.astype(np.float32))


def prep_inputs(inp):
    bf = ml_dtypes.bfloat16
    f32 = np.float32
    g = lambda k: np.asarray(inp[k], dtype=f32)
    d = {}
    d["w_in_t"] = _tile_k_o(g("L0_attn_w_in"), 128, 512)
    wx = g("L0_attn_w_x")
    wxp = np.zeros((DI, 128), f32)
    wxp[:, 0:16] = wx[:, RK:RK + NS]
    wxp[:, 32:48] = wx[:, RK + NS:RK + 2 * NS]
    wxp[:, 64:128] = wx[:, 0:RK]
    d["w_x_t"] = np.ascontiguousarray(wxp.reshape(16, 128, 128))
    d["w_dt_t"] = np.ascontiguousarray(g("L0_attn_w_dt"))
    d["w_out_t"] = _tile_k_o(g("L0_attn_w_out"), 128, 256)
    wf = g("L0_attn_w_ff1").reshape(8, 128, 32, 256)
    d["w_ff1_t"] = np.ascontiguousarray(
        np.stack([wf[:, :, 0:16], wf[:, :, 16:32]], axis=3)
        .transpose(2, 0, 3, 1, 4))
    d["w_ff2_t"] = _tile_k_o(g("L0_attn_w_ff2"), 128, 128)
    d["w1_t"] = _tile_k_o(np.ascontiguousarray(g("L0_ffn_w1_w").T), 128, 256, bf)
    d["w2_t"] = _tile_k_o(np.ascontiguousarray(g("L0_ffn_w2_w").T), 128, 256, bf)
    d["h1_t"] = _tile_k_o(np.ascontiguousarray(g("L0_ffn_h1_w").T), 128, 256, bf)
    d["h2_t"] = _tile_k_o(np.ascontiguousarray(g("L0_ffn_h2_w").T), 128, 256, bf)
    d["w3_t"] = _tile_k_o(np.ascontiguousarray(g("L0_ffn_w3_w").T), 128, 256, bf)
    d["ln1_g"] = _perpart(g("L0_attn_ln1_g"), 8)
    d["ln1_b"] = _perpart(g("L0_attn_ln1_b"), 8)
    d["ln2_g"] = _perpart(g("L0_attn_ln2_g"), 8)
    d["ln2_b"] = _perpart(g("L0_attn_ln2_b"), 8)
    d["conv_w"] = np.ascontiguousarray(
        g("L0_attn_conv_w").reshape(16, 128, 4).transpose(1, 0, 2))
    d["conv_b"] = _perpart(g("L0_attn_conv_b"), 16)
    d["b_dt"] = _perpart(g("L0_attn_b_dt"), 16)
    d["d_skip"] = _perpart(g("L0_attn_d_skip"), 16)
    d["a_log"] = np.ascontiguousarray(
        g("L0_attn_a_log").reshape(16, 128, NS).transpose(1, 0, 2))
    d["b_ff1"] = _perpart(g("L0_attn_b_ff1"), 64)
    d["b_ff2"] = _perpart(g("L0_attn_b_ff2"), 8)
    d["w1_s"] = _perpart(g("L0_ffn_w1_s"), 32)
    d["w2_s"] = _perpart(g("L0_ffn_w2_s"), 32)
    d["h1_s"] = _perpart(g("L0_ffn_h1_s"), 32)
    d["h2_s"] = _perpart(g("L0_ffn_h2_s"), 32)
    d["w3_s"] = _perpart(g("L0_ffn_w3_s"), 8)
    return d


SMALL_SHAPES = [("ln1_g", [128, 8]), ("ln1_b", [128, 8]), ("ln2_g", [128, 8]),
                ("ln2_b", [128, 8]), ("conv_w", [128, 16, 4]), ("conv_b", [128, 16]),
                ("b_dt", [128, 16]), ("d_skip", [128, 16]), ("a_log", [128, 16, NS]),
                ("b_ff1", [128, 64]), ("b_ff2", [128, 8]), ("w1_s", [128, 32]),
                ("w2_s", [128, 32]), ("h1_s", [128, 32]), ("h2_s", [128, 32]),
                ("w3_s", [128, 8])]



def build_program(n_slab=NSEQ * L // SLAB, stage_upto=99, debug_dump=()):
    nc = bass.Bass("TRN2", target_bir_lowering=False)
    dbg_names = set(debug_dump)

    x_in = nc.dram_tensor("x", [NSEQ * L, D], F32, kind="ExternalInput").ap()
    x_out = nc.dram_tensor("out", [NSEQ * L, D], F32, kind="ExternalOutput").ap()
    w_in_t = nc.dram_tensor("w_in_t", [8, 8, 128, 512], F32, kind="ExternalInput").ap()
    w_x_t = nc.dram_tensor("w_x_t", [16, 128, 128], F32, kind="ExternalInput").ap()
    w_dt_t = nc.dram_tensor("w_dt_t", [64, 2048], F32, kind="ExternalInput").ap()
    w_out_t = nc.dram_tensor("w_out_t", [4, 16, 128, 256], F32, kind="ExternalInput").ap()
    w_ff1_t = nc.dram_tensor("w_ff1_t", [16, 8, 2, 128, 256], F32, kind="ExternalInput").ap()
    w_ff2_t = nc.dram_tensor("w_ff2_t", [8, 32, 128, 128], F32, kind="ExternalInput").ap()
    w1_t = nc.dram_tensor("w1_t", [16, 8, 128, 256], BF16, kind="ExternalInput").ap()
    w2_t = nc.dram_tensor("w2_t", [16, 8, 128, 256], BF16, kind="ExternalInput").ap()
    h1_t = nc.dram_tensor("h1_t", [16, 32, 128, 256], BF16, kind="ExternalInput").ap()
    h2_t = nc.dram_tensor("h2_t", [16, 32, 128, 256], BF16, kind="ExternalInput").ap()
    w3_t = nc.dram_tensor("w3_t", [4, 32, 128, 256], BF16, kind="ExternalInput").ap()
    smalls = {nm: nc.dram_tensor(nm, shp, F32, kind="ExternalInput").ap()
              for nm, shp in SMALL_SHAPES}

    with contextlib.ExitStack() as ctx:
        e = ctx.enter_context
        sb = lambda nm, shape, dt_: e(nc.sbuf_tensor(nm, shape, dt_))

        sm = {nm: sb("s_" + nm, shp, F32) for nm, shp in SMALL_SHAPES}
        A_t = sb("A_t", [128, 16, NS], F32)
        iop = sb("iop", [128, 1], F32)
        cst = sb("cst", [128, 2], F32)
        sel_bf = sb("sel_bf", [16, 2048], BF16)
        id_f32 = sb("id_f32", [128, 128], F32)
        id_f32r = sb("id_f32r", [128, 128], F32R)
        id_bf16 = sb("id_bf16", [128, 128], BF16)
        fscrap = sb("fscrap", [128, 512], F32)
        xwork = sb("xwork", [128, NT, 1024], F32)
        stat_all = sb("stat", [128, NCHUNK, 12], F32)
        statw = sb("statw", [128, NT, 24], F32)
        h_state = sb("h_state", [128, 16, NS], F32)
        halo = sb("halo", [128, 16, 3], F32)
        scr = e(Scratch(nc, "main", 128 * 1024))
        scrr = e(Scratch(nc, "f32r", 40 * 1024, F32R))

        psA = e(nc.psum_tensor("psA", [128, 2048], F32))
        psB = e(nc.psum_tensor("psB", [128, 1024], F32))
        psC = e(nc.psum_tensor("psC", [128, 512], F32))
        psD = e(nc.psum_tensor("psD", [128, 512], F32))

        p = Prog()

        def fence(t):
            return t.matmul(psC[0:1, 256:384], id_bf16[:, 0:1], id_bf16[:],
                            start=True, stop=True)

        T = Trk
        T_sm = {nm: T() for nm in sm}
        T_statc = [T() for _ in range(NCHUNK)]
        T_A, T_sel, T_id, T_iop = T(), T(), T(), T()
        T_xwork = [T() for _ in range(NT)]
        T_statw, T_hst, T_halo = T(), T(), T()
        T_psA, T_psB, T_psC, T_psD = T(), T(), T(), T()
        T_xout = T()

        out_evs = []
        dbg_handles = {}

        def dbg(nm, src_ap, reads, shape, dt_=F32):
            if nm not in dbg_names:
                return
            if src_ap.dtype == F32R:
                src_ap = src_ap.bitcast(F32)
            t = nc.dram_tensor("dbg_" + nm, shape, dt_, kind="ExternalOutput").ap()
            dbg_handles[nm] = t
            ev = p.task("sp", lambda s, a=src_ap, o=t: s.dma_start(out=o, in_=a),
                        reads=reads, dma=True, dkey="dbg")
            out_evs.append(ev)

        sm_ev = None
        for nm, ap in smalls.items():
            sm_ev = p.task("sp", lambda s, ap=ap, nm=nm: s.dma_start(
                out=sm[nm][:], in_=ap), writes=[T_sm[nm]], dma=True, dkey="sm")
        for nm in sm:
            T_sm[nm].w = sm_ev

        io_scr = scr.alloc([2304], F32)

        def f_iota(g):
            g.iota(io_scr[0:128, 0:128], [[1, 128]],
                   channel_multiplier=0, allow_small_or_imprecise_dtypes=True)
            g.iota(iop[:], [[0, 1]], channel_multiplier=1,
                   allow_small_or_imprecise_dtypes=True)
            return g.iota(io_scr[0:16, 128:2176], [[1, 16], [0, 128]],
                          channel_multiplier=0, allow_small_or_imprecise_dtypes=True)
        ev_io = p.emit("pool", f_iota)
        T_iop.did_write(ev_io)

        def f_ids(g):
            g.tensor_scalar(out=id_f32[:], in0=io_scr[0:128, 0:128],
                            scalar1=iop[:], scalar2=None, op0=ALU.is_equal)
            g.tensor_scalar(out=id_f32r[:], in0=io_scr[0:128, 0:128], scalar1=iop[:],
                            scalar2=None, op0=ALU.is_equal)
            g.tensor_scalar(out=id_bf16[:], in0=io_scr[0:128, 0:128], scalar1=iop[:],
                            scalar2=None, op0=ALU.is_equal)
            return g.tensor_scalar(out=sel_bf[:], in0=io_scr[0:16, 128:2176],
                                   scalar1=iop[0:16, :], scalar2=None, op0=ALU.is_equal)
        ev_ids = p.emit("pool", f_ids, deps=[ev_io])
        T_id.did_write(ev_ids)
        T_sel.did_write(ev_ids)
        scr.events.append(ev_io)
        scr.events.append(ev_ids)

        T_cst = T()
        p.task("dve", lambda v: v.memset(cst[:, 0:1], EPS), writes=[T_cst])
        p.task("dve", lambda v: v.memset(cst[:, 1:2], 1.0), writes=[T_cst])
        p.task("act", lambda act: act.activation(
            out=A_t[:].rearrange("p a n -> p (a n)"),
            in_=sm["a_log"][:].rearrange("p a n -> p (a n)"), func=AF.Exp),
            reads=[T_sm["a_log"]], writes=[T_A])
        p.task("dve", lambda v: v.tensor_scalar_mul(
            A_t[:].rearrange("p a n -> p (a n)"),
            A_t[:].rearrange("p a n -> p (a n)"), -1.0),
            reads=[T_A], writes=[T_A])

        scr.off = 0
        scrr.off = 0
        x0c = scr.alloc([NT, 1024], F32)
        tmpA = scr.alloc([1024], F32)
        h_tm = scrr.alloc([1024], F32R)
        h_fm = scrr.alloc([8, TC], F32R)
        xc_raw = scr.alloc([16, 3 + TC], F32)
        xc_act = scrr.alloc([16, TC], F32R)
        z_bf = scr.alloc([16, TC], BF16)
        dt_f = scr.alloc([16, TC], F32)
        sp_t1 = scr.alloc([16, TC], F32)
        dtlo_r = scrr.alloc([TC], F32R, parts=64)
        b_sb = scr.alloc([TC], BF16, parts=16)
        c_sb = scr.alloc([TC], BF16, parts=16)
        dA = scr.alloc([NS, TC], F32)
        u_bf = scr.alloc([NS, TC], BF16)
        hall = scr.alloc([NS, TC], BF16)
        brep = scr.alloc([NS, TC], BF16)
        crep = scr.alloc([NS, TC], BF16)
        dtx = scr.alloc([TC], F32)
        ytree = scr.alloc([NS, TC], BF16)
        ygate = scr.alloc([16, TC], BF16)
        gtmp = scr.alloc([TC], F32)
        m_fm = scr.alloc([8, TC], F32)
        wb_in = [scrr.alloc([512], F32R) for _ in range(2)]
        wb_wx = scrr.alloc([16, 128], F32R)
        wb_wdt = scrr.alloc([2048], F32R, parts=64)
        wb_out = [scr.alloc([16, 256], BF16) for _ in range(2)]
        scrB_end = scr.off
        scrrB_end = scrr.off

        scr.off = 0
        h2tm = scr.alloc([1024], BF16)
        h2fm = scr.alloc([8, SLAB], BF16)
        wb_ff1 = [scr.alloc([8, 2, 256], BF16) for _ in range(2)]
        a_sb = scr.alloc([2, SLAB], BF16)
        g_sb = scr.alloc([2, SLAB], BF16)
        h2b = scr.alloc([32, SLAB], BF16)
        wb_ff2 = [scr.alloc([32, 128], BF16) for _ in range(2)]
        gg_fm = scr.alloc([8, SLAB], F32)
        scrC_end = scr.off

        scr.off = 0
        hbuf = scr.alloc([32, SLAB], BF16)
        h2buf = scr.alloc([32, SLAB], BF16)
        wb_e = [scr.alloc([8192], BF16) for _ in range(2)]
        offE = scr.off
        xd_fm = scr.alloc([8, SLAB], BF16)
        s_sb = scr.alloc([2, SLAB], BF16)
        t_sb = scr.alloc([2, SLAB], BF16)
        xdtm = scr.alloc([1024], BF16)
        endE1 = scr.off
        scr.off = offE
        f_fm = scr.alloc([8, SLAB], F32)
        scrE_end = max(scr.off, endE1)
        assert scrC_end <= 112 * 1024, scrC_end
        assert scrE_end <= 112 * 1024, scrE_end

        scr.off = 112 * 1024
        x0c2 = scr.alloc([1024], F32)
        tmpD = scr.alloc([1024], F32)
        ftmp = scr.alloc([1024], F32)
        scrD_end = scr.off
        assert scrD_end <= 128 * 1024, scrD_end

        for slab in range(n_slab):
            tok0 = (slab // (L // SLAB)) * L + (slab % (L // SLAB)) * SLAB
            seq_start = (slab % (L // SLAB) == 0)

            scr.reset()
            scrr.reset()

            T_x0c, T_tmpA, T_htm, T_hfm = T(), T(), T(), T()
            T_xcraw, T_xcact, T_zbf = T(), T(), T()
            T_dt, T_sp1, T_dtlo, T_bsb, T_csb = T(), T(), T(), T(), T()
            T_dA, T_u, T_hall, T_brep, T_crep = T(), T(), T(), T(), T()
            T_dtx, T_ytree, T_ygate, T_gtmp, T_mfm = T(), T(), T(), T(), T()
            T_win = [T(), T()]
            T_wx, T_wdt = T(), T()
            T_wout = [T(), T()]
            tsk = lambda *a, **k: scr_task(p, (scr, scrr), *a, **k)

            tsk("pool", lambda g: g.dma_start(
                out=wb_wx, in_=w_x_t.rearrange("s p c -> p s c")),
                writes=[T_wx], dma=True, dkey="wx")
            tsk("pool", lambda g: g.dma_start(out=wb_wdt, in_=w_dt_t),
                writes=[T_wdt], dma=True, dkey="wdt")
            tsk("sp", lambda s, r=tok0: s.dma_start(
                out=x0c,
                in_=x_in[r:r + SLAB, :].rearrange("(a p) d -> p a d", p=128)),
                writes=[T_x0c], dma=True, dkey="x0")

            dbg(f"x0c_{slab}", x0c.rearrange("p a d -> p (a d)"), [T_x0c],
                [128, NT * 1024])
            if seq_start:
                p.task("dve", lambda v: v.memset(
                    halo[:].rearrange("p s t -> p (s t)"), 0.0), writes=[T_halo])
                p.task("dve", lambda v: v.memset(
                    h_state[:].rearrange("p s n -> p (s n)"), 0.0), writes=[T_hst])

            for c in range(NCHUNK):
                stat = stat_all[:, c, :]
                T_stat = T_statc[c]
                def f_st0(v, c=c, stat=stat):
                    return v.tensor_reduce(out=stat[:, 0:1], in_=x0c[:, c, :],
                                           axis=AX.X, op=ALU.add)
                tsk("dve", f_st0, reads=[T_x0c], writes=[T_stat])

                def f_st0b(v, stat=stat):
                    return v.tensor_scalar(out=stat[:, 1:2], in0=stat[:, 0:1],
                                           scalar1=1.0 / D, scalar2=None, op0=ALU.mult)
                tsk("dve", f_st0b, reads=[T_stat], writes=[T_stat])
                dbg(f"stA_{slab}_{c}", stat, [T_stat], [128, 12])

                def f_st1(act, c=c, stat=stat):
                    return act.activation(out=tmpA, in_=x0c[:, c, :], func=AF.Square,
                                          scale=-1.0, bias=stat[:, 1:2],
                                          accum_out=stat[:, 2:3])
                tsk("act", f_st1, reads=[T_x0c, T_stat], writes=[T_tmpA, T_stat])
                dbg(f"stB_{slab}_{c}", stat, [T_stat], [128, 12])

                def f_st2(act, stat=stat):
                    act.activation(out=stat[:, 3:4], in_=stat[:, 2:3], func=AF.Ln,
                                   scale=1.0 / D, bias=cst[:, 0:1])
                    act.activation(out=stat[:, 4:5], in_=stat[:, 3:4], func=AF.Exp,
                                   scale=-0.5)
                    return act.activation(out=stat[:, 5:6], in_=stat[:, 3:4],
                                          func=AF.Exp)
                tsk("act", f_st2, reads=[T_stat, T_cst], writes=[T_stat])
                dbg(f"stC_{slab}_{c}", stat, [T_stat], [128, 12])

                def f_st3(v, stat=stat):
                    return v.reciprocal(out=stat[:, 11:12], in_=stat[:, 5:6])
                tsk("dve", f_st3, reads=[T_stat], writes=[T_stat])

                def f_st3b(v, stat=stat):
                    return v.tensor_scalar(out=stat[:, 6:7], in0=stat[:, 2:3],
                                           scalar1=1.0 / D, scalar2=stat[:, 11:12],
                                           op0=ALU.mult, op1=ALU.mult)
                tsk("dve", f_st3b, reads=[T_stat], writes=[T_stat])

                def f_st4(act, stat=stat):
                    act.activation(out=stat[:, 7:8], in_=stat[:, 6:7], func=AF.Ln,
                                   bias=cst[:, 0:1])
                    return act.activation(out=stat[:, 7:8], in_=stat[:, 7:8],
                                          func=AF.Exp, scale=-0.5)
                tsk("act", f_st4, reads=[T_stat, T_cst], writes=[T_stat])

                def f_st5(v, stat=stat):
                    v.tensor_scalar(out=stat[:, 8:9], in0=stat[:, 4:5],
                                    scalar1=stat[:, 7:8], scalar2=None, op0=ALU.mult)
                    return v.tensor_scalar(out=stat[:, 10:11], in0=stat[:, 1:2],
                                           scalar1=stat[:, 4:5], scalar2=-1.0,
                                           op0=ALU.mult, op1=ALU.mult)
                tsk("dve", f_st5, reads=[T_stat], writes=[T_stat])

                def f_st5b(v, stat=stat):
                    return v.tensor_scalar(out=stat[:, 9:10], in0=stat[:, 1:2],
                                           scalar1=stat[:, 8:9], scalar2=-1.0,
                                           op0=ALU.mult, op1=ALU.mult)
                tsk("dve", f_st5b, reads=[T_stat], writes=[T_stat])

                def f_htm(act, c=c, stat=stat):
                    return act.activation(out=h_tm, in_=x0c[:, c, :], func=AF.Identity,
                                          scale=stat[:, 8:9], bias=stat[:, 9:10])
                dbg(f"stat_{slab}_{c}", stat, [T_stat], [128, 12])
                tsk("act", f_htm, reads=[T_x0c, T_stat], writes=[T_htm])
                dbg(f"htm_{slab}_{c}", h_tm, [T_htm], [128, 1024])

                def f_htr(t):
                    last = None
                    for ds in range(8):
                        last = t.transpose(
                            psA[:, ds * 256:ds * 256 + 128].bitcast(F32R),
                            h_tm[:, ds * 128:(ds + 1) * 128], id_f32r[:])
                    return fence(t)
                tsk("pe", f_htr, reads=[T_htm, T_id], writes=[T_psA])

                def f_hev(act):
                    last = None
                    for ds in range(8):
                        last = act.activation(
                            out=h_fm[:, ds, :],
                            in_=psA[:, ds * 256:ds * 256 + 128].bitcast(F32R),
                            func=AF.Identity, scale=sm["ln1_g"][:, ds:ds + 1],
                            bias=sm["ln1_b"][:, ds:ds + 1])
                    return last
                tsk("act", f_hev, reads=[T_psA, T_sm["ln1_g"], T_sm["ln1_b"]],
                    writes=[T_hfm])
                dbg(f"hfm_{slab}_{c}", h_fm.rearrange("p a t -> p (a t)"),
                    [T_hfm], [128, 8 * TC])

                for o8 in range(8):
                    for k in range(8):
                        wbuf, tw = wb_in[k % 2], T_win[k % 2]
                        tsk("pool", lambda g, o8=o8, k=k, wbuf=wbuf: g.dma_start(
                            out=wbuf, in_=w_in_t[o8, k]), writes=[tw], dma=True,
                            dkey=f"win{k % 2}")

                        def f_mm(t, k=k, wbuf=wbuf):
                            last = None
                            for ot in range(4):
                                last = t.matmul(
                                    psA[:, ot * 512:ot * 512 + 128],
                                    wbuf[:, ot * 128:(ot + 1) * 128],
                                    h_fm[:, k, :], start=(k == 0), stop=(k == 7))
                            if k == 7:
                                last = fence(t)
                            return last
                        tsk("pe", f_mm, reads=[tw, T_hfm],
                            writes=([T_psA] if k == 7 else []),
                            extra=(T_psA.wr() if k == 0 else []))
                    if o8 < 4:
                        def f_evq(act, o8=o8):
                            last = None
                            for ot in range(4):
                                last = act.activation(
                                    out=xc_raw[:, o8 * 4 + ot, 3:3 + TC],
                                    in_=psA[:, ot * 512:ot * 512 + 128], func=AF.Copy)
                            return last
                        tsk("act", f_evq, reads=[T_psA], writes=[T_xcraw])
                    else:
                        def f_evz(act, o8=o8):
                            last = None
                            for ot in range(4):
                                last = act.activation(
                                    out=z_bf[:, (o8 - 4) * 4 + ot, :],
                                    in_=psA[:, ot * 512:ot * 512 + 128], func=AF.Silu)
                            return last
                        tsk("act", f_evz, reads=[T_psA], writes=[T_zbf])

                dbg(f"xcraw_{slab}_{c}",
                    xc_raw[:, :, 3:3 + TC].rearrange("p s t -> p s t"),
                    [T_xcraw], [128, 16, TC])

                tsk("dve", lambda v: v.tensor_copy(
                    out=xc_raw[:, :, 0:3], in_=halo[:]),
                    reads=[T_halo], writes=[T_xcraw])

                def f_conv(v):
                    last = None
                    for s16 in range(16):
                        v.tensor_scalar(out=sp_t1[:, s16, :], in0=xc_raw[:, s16, 3:3 + TC],
                                        scalar1=sm["conv_w"][:, s16, 3:4], scalar2=None,
                                        op0=ALU.mult)
                        for j in range(3):
                            last = v.scalar_tensor_tensor(
                                out=sp_t1[:, s16, :], in0=xc_raw[:, s16, j:j + TC],
                                scalar=sm["conv_w"][:, s16, j:j + 1],
                                in1=sp_t1[:, s16, :], op0=ALU.mult, op1=ALU.add)
                    return last
                tsk("dve", f_conv, reads=[T_xcraw, T_sm["conv_w"]], writes=[T_sp1])

                tsk("dve", lambda v: v.tensor_copy(
                    out=halo[:], in_=xc_raw[:, :, TC:TC + 3]),
                    reads=[T_xcraw], writes=[T_halo])

                def f_silu(act):
                    last = None
                    for s16 in range(16):
                        last = act.activation(out=xc_act[:, s16, :], in_=sp_t1[:, s16, :],
                                              func=AF.Silu,
                                              bias=sm["conv_b"][:, s16:s16 + 1])
                    return last
                tsk("act", f_silu, reads=[T_sp1, T_sm["conv_b"]], writes=[T_xcact])
                dbg(f"xcact_{slab}_{c}", xc_act.rearrange("p s t -> p (s t)"),
                    [T_xcact], [128, 16 * TC])

                def f_dbc(t):
                    last = None
                    for k in range(16):
                        last = t.matmul(psC[:, 0:TC], wb_wx[:, k, :], xc_act[:, k, :],
                                        start=(k == 0), stop=(k == 15))
                    return fence(t)
                tsk("pe", f_dbc, reads=[T_wx, T_xcact], writes=[T_psC])

                def f_dbc_ev(act):
                    act.activation(out=b_sb, in_=psC[0:16, 0:TC], func=AF.Copy)
                    act.activation(out=c_sb, in_=psC[32:48, 0:TC], func=AF.Copy)
                    return act.activation(out=dtlo_r, in_=psC[64:128, 0:TC],
                                          func=AF.Copy)
                tsk("act", f_dbc_ev, reads=[T_psC], writes=[T_bsb, T_csb, T_dtlo])

                for grp in range(2):
                    def f_wdt(t, grp=grp):
                        last = None
                        for j in range(8):
                            ds = grp * 8 + j
                            last = t.matmul(psB[:, j * 128:(j + 1) * 128],
                                            wb_wdt[:, ds * 128:(ds + 1) * 128],
                                            dtlo_r, start=True, stop=True)
                        return fence(t)
                    tsk("pe", f_wdt, reads=[T_wdt, T_dtlo], writes=[T_psB])

                    def f_sp(act, grp=grp):
                        last = None
                        for j in range(8):
                            ds = grp * 8 + j
                            pb = psB[:, j * 128:(j + 1) * 128]
                            act.activation(out=sp_t1[:, ds, :], in_=pb, func=AF.Abs,
                                           bias=sm["b_dt"][:, ds:ds + 1])
                            act.activation(out=dt_f[:, ds, :], in_=pb, func=AF.Relu,
                                           bias=sm["b_dt"][:, ds:ds + 1])
                            act.activation(out=sp_t1[:, ds, :], in_=sp_t1[:, ds, :],
                                           func=AF.Exp, scale=-1.0)
                            last = act.activation(out=sp_t1[:, ds, :],
                                                  in_=sp_t1[:, ds, :], func=AF.Ln,
                                                  bias=cst[:, 1:2])
                        return last
                    tsk("act", f_sp, reads=[T_psB, T_sm["b_dt"], T_cst],
                        writes=[T_sp1, T_dt])

                tsk("dve", lambda v: v.tensor_add(
                    dt_f.rearrange("p s t -> p (s t)"),
                    dt_f.rearrange("p s t -> p (s t)"),
                    sp_t1.rearrange("p s t -> p (s t)")),
                    reads=[T_sp1, T_dt], writes=[T_dt])
                dbg(f"dt_{slab}_{c}", dt_f.rearrange("p s t -> p (s t)"),
                    [T_dt], [128, 16 * TC])

                def f_bcB(t):
                    last = None
                    for n in range(NS):
                        last = t.matmul(psA[:, n * TC:(n + 1) * TC],
                                        sel_bf[:, n * 128:(n + 1) * 128], b_sb,
                                        start=True, stop=True)
                    return fence(t)
                tsk("pe", f_bcB, reads=[T_sel, T_bsb], writes=[T_psA])
                tsk("act", lambda act: act.activation(
                    out=brep.rearrange("p n t -> p (n t)"),
                    in_=psA[:, 0:2048], func=AF.Copy),
                    reads=[T_psA], writes=[T_brep])

                def f_bcC(t):
                    last = None
                    for n in range(NS):
                        last = t.matmul(psA[:, n * TC:(n + 1) * TC],
                                        sel_bf[:, n * 128:(n + 1) * 128], c_sb,
                                        start=True, stop=True)
                    return fence(t)
                tsk("pe", f_bcC, reads=[T_sel, T_csb], writes=[T_psA])
                tsk("act", lambda act: act.activation(
                    out=crep.rearrange("p n t -> p (n t)"),
                    in_=psA[:, 0:2048], func=AF.Copy),
                    reads=[T_psA], writes=[T_crep])

                for ds in range(16):
                    def f_e(v, ds=ds):
                        a_b = bc(A_t[:, ds, :], 2, TC)
                        d_b = bc(dt_f[:, ds, :], 1, NS)
                        return v.tensor_tensor(out=dA[:], in0=a_b, in1=d_b,
                                               op=ALU.mult)
                    tsk("dve", f_e, reads=[T_A, T_dt], writes=[T_dA])

                    tsk("act", lambda act: act.activation(
                        out=dA.rearrange("p n t -> p (n t)"),
                        in_=dA.rearrange("p n t -> p (n t)"), func=AF.Exp),
                        reads=[T_dA], writes=[T_dA])

                    def f_dtx(v, ds=ds):
                        return v.tensor_mul(dtx, dt_f[:, ds, :],
                                            xc_act[:, ds, :].bitcast(F32))
                    tsk("dve", f_dtx, reads=[T_dt, T_xcact], writes=[T_dtx])

                    def f_u(v):
                        return v.tensor_tensor(out=u_bf[:], in0=brep[:],
                                               in1=bc(dtx, 1, NS), op=ALU.mult)
                    tsk("dve", f_u, reads=[T_dtx, T_brep], writes=[T_u])

                    def f_scan(v, ds=ds):
                        last = None
                        for n in range(NS):
                            last = v.tensor_tensor_scan(
                                out=hall[:, n, :], data0=dA[:, n, :],
                                data1=u_bf[:, n, :],
                                initial=h_state[:, ds, n:n + 1],
                                op0=ALU.mult, op1=ALU.add)
                        return last
                    tsk("dve", f_scan, reads=[T_dA, T_u, T_hst], writes=[T_hall])

                    def f_hst(v, ds=ds):
                        return v.tensor_copy(out=h_state[:, ds, :],
                                             in_=hall[:, :, TC - 1])
                    tsk("dve", f_hst, reads=[T_hall], writes=[T_hst])

                    tsk("dve", lambda v: v.tensor_mul(
                        ytree.rearrange("p n t -> p (n t)"),
                        hall.rearrange("p n t -> p (n t)"),
                        crep.rearrange("p n t -> p (n t)")),
                        reads=[T_hall, T_crep], writes=[T_ytree])

                    def f_yt(v):
                        last = None
                        for hn in (8, 4, 2, 1):
                            last = v.tensor_add(
                                ytree[:, 0:hn, :].rearrange("p n t -> p (n t)"),
                                ytree[:, 0:hn, :].rearrange("p n t -> p (n t)"),
                                ytree[:, hn:2 * hn, :].rearrange("p n t -> p (n t)"))
                        return last
                    tsk("dve", f_yt, reads=[T_ytree], writes=[T_ytree])

                    def f_g1(v, ds=ds):
                        return v.scalar_tensor_tensor(
                            out=gtmp, in0=xc_act[:, ds, :].bitcast(F32),
                            scalar=sm["d_skip"][:, ds:ds + 1], in1=ytree[:, 0, :],
                            op0=ALU.mult, op1=ALU.add)
                    tsk("dve", f_g1, reads=[T_xcact, T_sm["d_skip"], T_ytree],
                        writes=[T_gtmp])

                    def f_g2(v, ds=ds):
                        return v.tensor_mul(ygate[:, ds, :], gtmp, z_bf[:, ds, :])
                    tsk("dve", f_g2, reads=[T_gtmp, T_zbf], writes=[T_ygate])

                dbg(f"ygate_{slab}_{c}", ygate.rearrange("p s t -> p (s t)"),
                    [T_ygate], [128, 16 * TC], BF16)

                for oqr in range(4):
                    wbuf, tw = wb_out[oqr % 2], T_wout[oqr % 2]
                    tsk("pool", lambda g, oqr=oqr, wbuf=wbuf: g.dma_start(
                        out=wbuf, in_=w_out_t[oqr].rearrange("s p t -> p s t")),
                        writes=[tw], dma=True, dkey=f"wout{oqr % 2}")

                    def f_wout(t, wbuf=wbuf):
                        last = None
                        for hf in range(2):
                            for k in range(16):
                                last = t.matmul(psD[:, hf * 128:(hf + 1) * 128],
                                                wbuf[:, k, hf * 128:(hf + 1) * 128],
                                                ygate[:, k, :], start=(k == 0),
                                                stop=(k == 15))
                        return fence(t)
                    tsk("pe", f_wout, reads=[tw, T_ygate], writes=[T_psD])

                    def f_mev(act, oqr=oqr):
                        last = None
                        for hf in range(2):
                            last = act.activation(out=m_fm[:, oqr * 2 + hf, :],
                                                  in_=psD[:, hf * 128:(hf + 1) * 128],
                                                  func=AF.Copy)
                        return last
                    tsk("act", f_mev, reads=[T_psD], writes=[T_mfm])

                dbg(f"mfm_{slab}_{c}", m_fm.rearrange("p a t -> p (a t)"),
                    [T_mfm], [128, 8 * TC])

                def f_mtr(t):
                    last = None
                    for ds in range(8):
                        last = t.transpose(psB[:, ds * 128:(ds + 1) * 128],
                                           m_fm[:, ds, :], id_f32[:])
                    return fence(t)
                tsk("pe", f_mtr, reads=[T_mfm, T_id], writes=[T_psB])

                def f_xa(act, c=c, stat=stat):
                    return act.activation(out=tmpA, in_=x0c[:, c, :], func=AF.Identity,
                                          scale=stat[:, 4:5], bias=stat[:, 10:11])
                tsk("act", f_xa, reads=[T_x0c, T_stat], writes=[T_tmpA])

                def f_xb(v, c=c):
                    return v.scalar_tensor_tensor(
                        out=xwork[:, c, :], in0=psB[:, 0:1024], scalar=1.0,
                        in1=tmpA, op0=ALU.mult, op1=ALU.add,
                        accum_out=statw[:, c, 0:1])
                tsk("dve", f_xb, reads=[T_psB, T_tmpA],
                    writes=[T_xwork[c], T_statw])
                dbg(f"xb_{slab}_{c}", xwork[:, c, :], [T_xwork[c]], [128, 1024])

            if stage_upto < 2:
                continue

            def ln_stats(ti, base, src_tile, have_sum, tmp_out, T_tmp):
                sw = statw[:, ti, :]
                if not have_sum:
                    tsk("dve", lambda v, s=src_tile, sw=sw: v.tensor_reduce(
                        out=sw[:, base:base + 1], in_=s, axis=AX.X, op=ALU.add),
                        reads=[T_statw], writes=[T_statw])
                tsk("dve", lambda v, sw=sw: v.tensor_scalar(
                    out=sw[:, base + 2:base + 3], in0=sw[:, base:base + 1],
                    scalar1=1.0 / D, scalar2=None, op0=ALU.mult),
                    reads=[T_statw], writes=[T_statw])
                tsk("act", lambda a, s=src_tile, sw=sw: a.activation(
                    out=tmp_out, in_=s, func=AF.Square, scale=-1.0,
                    bias=sw[:, base + 2:base + 3],
                    accum_out=sw[:, base + 1:base + 2]),
                    reads=[T_statw], writes=[T_statw, T_tmp])
                def f_r(a, sw=sw):
                    a.activation(out=sw[:, base + 3:base + 4],
                                 in_=sw[:, base + 1:base + 2], func=AF.Ln,
                                 scale=1.0 / D, bias=cst[:, 0:1])
                    return a.activation(out=sw[:, base + 3:base + 4],
                                        in_=sw[:, base + 3:base + 4], func=AF.Exp,
                                        scale=-0.5)
                tsk("act", f_r, reads=[T_statw, T_cst], writes=[T_statw])
                tsk("dve", lambda v, sw=sw: v.tensor_scalar(
                    out=sw[:, base + 4:base + 5], in0=sw[:, base + 2:base + 3],
                    scalar1=sw[:, base + 3:base + 4], scalar2=-1.0,
                    op0=ALU.mult, op1=ALU.mult),
                    reads=[T_statw], writes=[T_statw])

            scr.reset()
            scrr.reset()
            T_h2tm, T_h2fm, T_asb, T_gsb, T_h2b, T_ggfm = T(), T(), T(), T(), T(), T()
            T_tmpD0 = T()
            T_wff1 = [T(), T()]
            T_wff2 = [T(), T()]

            for ti in range(NT):
                ln_stats(ti, 0, xwork[:, ti, :], True, tmpD, T_tmpD0)
                sw = statw[:, ti, :]
                tsk("act", lambda a, ti=ti, sw=sw: a.activation(
                    out=h2tm, in_=xwork[:, ti, :], func=AF.Identity,
                    scale=sw[:, 3:4], bias=sw[:, 4:5]),
                    reads=[T_xwork[ti], T_statw], writes=[T_h2tm])

                def f_h2tr(t):
                    pa = psA.bitcast(BF16)
                    last = None
                    for ds in range(8):
                        last = t.transpose(pa[:, ds * 128:(ds + 1) * 128],
                                           h2tm[:, ds * 128:(ds + 1) * 128],
                                           id_bf16[:])
                    return fence(t)
                tsk("pe", f_h2tr, reads=[T_h2tm, T_id], writes=[T_psA])

                def f_h2ev(a, ti=ti):
                    pa = psA.bitcast(BF16)
                    last = None
                    for ds in range(8):
                        last = a.activation(
                            out=h2fm[:, ds, ti * 128:(ti + 1) * 128],
                            in_=pa[:, ds * 128:(ds + 1) * 128], func=AF.Identity,
                            scale=sm["ln2_g"][:, ds:ds + 1],
                            bias=sm["ln2_b"][:, ds:ds + 1])
                    return last
                tsk("act", f_h2ev, reads=[T_psA, T_sm["ln2_g"], T_sm["ln2_b"]],
                    writes=[T_h2fm])

            for og in range(16):
                wbuf, tw = wb_ff1[og % 2], T_wff1[og % 2]
                tsk("pool", lambda g, og=og, wbuf=wbuf: g.dma_start(
                    out=wbuf, in_=w_ff1_t[og].rearrange("k h p t -> p k h t")),
                    writes=[tw], dma=True, dkey=f"ff1_{og % 2}")

                def f_ff1(t, wbuf=wbuf):
                    last = None
                    for half in range(2):
                        for ot in range(2):
                            for k in range(8):
                                last = t.matmul(
                                    psA[:, (half * 2 + ot) * 512:
                                        (half * 2 + ot) * 512 + 512],
                                    wbuf[:, k, half, ot * 128:(ot + 1) * 128],
                                    h2fm[:, k, :], start=(k == 0), stop=(k == 7))
                    return fence(t)
                tsk("pe", f_ff1, reads=[tw, T_h2fm], writes=[T_psA])

                def f_aev(a, og=og):
                    last = None
                    for ot in range(2):
                        last = a.activation(
                            out=a_sb[:, ot, :], in_=psA[:, ot * 512:ot * 512 + 512],
                            func=AF.Identity,
                            bias=sm["b_ff1"][:, og * 2 + ot:og * 2 + ot + 1])
                    for ot in range(2):
                        last = a.activation(
                            out=g_sb[:, ot, :],
                            in_=psA[:, 1024 + ot * 512:1024 + ot * 512 + 512],
                            func=AF.Gelu_apprx_tanh,
                            bias=sm["b_ff1"][:, 32 + og * 2 + ot:32 + og * 2 + ot + 1])
                    return last
                tsk("act", f_aev, reads=[T_psA, T_sm["b_ff1"]],
                    writes=[T_asb, T_gsb])

                def f_gmul(v, og=og):
                    return v.tensor_mul(
                        h2b[:, og * 2:og * 2 + 2, :].rearrange("p a t -> p (a t)"),
                        a_sb[:].rearrange("p a t -> p (a t)"),
                        g_sb[:].rearrange("p a t -> p (a t)"))
                tsk("dve", f_gmul, reads=[T_asb, T_gsb], writes=[T_h2b])

            for o2t in range(8):
                wbuf, tw = wb_ff2[o2t % 2], T_wff2[o2t % 2]
                tsk("pool", lambda g, o2t=o2t, wbuf=wbuf: g.dma_start(
                    out=wbuf, in_=w_ff2_t[o2t].rearrange("s p t -> p s t")),
                    writes=[tw], dma=True, dkey=f"ff2_{o2t % 2}")

                def f_ff2(t, wbuf=wbuf):
                    last = None
                    for k in range(32):
                        last = t.matmul(psB[:, 0:512], wbuf[:, k, :],
                                        h2b[:, k, :], start=(k == 0), stop=(k == 31))
                    return fence(t)
                tsk("pe", f_ff2, reads=[tw, T_h2b], writes=[T_psB])

                tsk("act", lambda a, o2t=o2t: a.activation(
                    out=gg_fm[:, o2t, :], in_=psB[:, 0:512], func=AF.Identity,
                    bias=sm["b_ff2"][:, o2t:o2t + 1]),
                    reads=[T_psB, T_sm["b_ff2"]], writes=[T_ggfm])

            for ti in range(NT):
                def f_ggtr(t, ti=ti):
                    last = None
                    for ds in range(8):
                        last = t.transpose(psB[:, ds * 128:(ds + 1) * 128],
                                           gg_fm[:, ds, ti * 128:(ti + 1) * 128],
                                           id_f32[:])
                    return fence(t)
                tsk("pe", f_ggtr, reads=[T_ggfm, T_id], writes=[T_psB])

                def f_xc2(v, ti=ti):
                    return v.scalar_tensor_tensor(
                        out=xwork[:, ti, :], in0=psB[:, 0:1024], scalar=1.0,
                        in1=xwork[:, ti, :], op0=ALU.mult, op1=ALU.add,
                        accum_out=statw[:, ti, 5:6])
                tsk("dve", f_xc2, reads=[T_psB], writes=[T_xwork[ti], T_statw])
                dbg(f"xc2_{slab}_{ti}", xwork[:, ti, :], [T_xwork[ti]], [128, 1024])

            if stage_upto < 3:
                continue

            scr.reset()
            T_x0c2, T_tmpD = T(), T()
            for ti in range(NT):
                ln_stats(ti, 5, xwork[:, ti, :], True, tmpD, T_tmpD)
                sw = statw[:, ti, :]
                tsk("sp", lambda s, r=tok0 + ti * 128: s.dma_start(
                    out=x0c2, in_=x_in[r:r + 128, :]),
                    writes=[T_x0c2], dma=True, dkey="x0b")
                tsk("act", lambda a, ti=ti, sw=sw: a.activation(
                    out=tmpD, in_=xwork[:, ti, :], func=AF.Identity,
                    scale=sw[:, 8:9], bias=sw[:, 9:10]),
                    reads=[T_xwork[ti], T_statw], writes=[T_tmpD])
                def f_x1(v, ti=ti):
                    return v.scalar_tensor_tensor(
                        out=xwork[:, ti, :], in0=tmpD, scalar=1.0, in1=x0c2,
                        op0=ALU.mult, op1=ALU.add,
                        accum_out=statw[:, ti, 10:11])
                tsk("dve", f_x1, reads=[T_tmpD, T_x0c2],
                    writes=[T_xwork[ti], T_statw])
                dbg(f"x1_{slab}_{ti}", xwork[:, ti, :], [T_xwork[ti]], [128, 1024])

            if stage_upto < 4:
                continue

            scr.reset()
            T_xdfm, T_hbuf, T_h2buf, T_ssb, T_tsb, T_ffm = T(), T(), T(), T(), T(), T()
            T_we = [T(), T()]
            T_we2 = [T(), T()]
            T_tmpD2, T_xdtm = T(), T()

            for ti in range(NT):
                ln_stats(ti, 10, xwork[:, ti, :], True, tmpD, T_tmpD2)
                sw = statw[:, ti, :]
                tsk("act", lambda a, ti=ti, sw=sw: a.activation(
                    out=xdtm, in_=xwork[:, ti, :], func=AF.Identity,
                    scale=sw[:, 13:14], bias=sw[:, 14:15]),
                    reads=[T_xwork[ti], T_statw], writes=[T_xdtm])

                def f_xdtr(t):
                    pa = psA.bitcast(BF16)
                    last = None
                    for ds in range(8):
                        last = t.transpose(pa[:, ds * 128:(ds + 1) * 128],
                                           xdtm[:, ds * 128:(ds + 1) * 128],
                                           id_bf16[:])
                    return fence(t)
                tsk("pe", f_xdtr, reads=[T_xdtm, T_id], writes=[T_psA])

                def f_xdev(a, ti=ti):
                    pa = psA.bitcast(BF16)
                    last = None
                    for ds in range(8):
                        last = a.activation(
                            out=xd_fm[:, ds, ti * 128:(ti + 1) * 128],
                            in_=pa[:, ds * 128:(ds + 1) * 128], func=AF.Copy)
                    return last
                tsk("act", f_xdev, reads=[T_psA], writes=[T_xdfm])

            for og in range(16):
                wbuf = wb_e[og % 2]
                tw1, tw2 = T_we[og % 2], T_we2[og % 2]
                w1v = wbuf[:, 0:2048].rearrange("p (k o) -> p k o", k=8)
                w2v = wbuf[:, 2048:4096].rearrange("p (k o) -> p k o", k=8)
                tsk("sp", lambda g, og=og, w1v=w1v: g.dma_start(
                    out=w1v, in_=w1_t[og].rearrange("s p t -> p s t")),
                    writes=[tw1], dma=True, dkey=f"te1_{og % 2}")
                tsk("sp", lambda g, og=og, w2v=w2v: g.dma_start(
                    out=w2v, in_=w2_t[og].rearrange("s p t -> p s t")),
                    writes=[tw2], dma=True, dkey=f"te2_{og % 2}")

                def f_l1(t, w1v=w1v, w2v=w2v):
                    last = None
                    for half in range(2):
                        wv = (w1v, w2v)[half]
                        for ot in range(2):
                            for k in range(8):
                                last = t.matmul(
                                    psA[:, (half * 2 + ot) * 512:
                                        (half * 2 + ot) * 512 + 512],
                                    wv[:, k, ot * 128:(ot + 1) * 128],
                                    xd_fm[:, k, :], start=(k == 0), stop=(k == 7))
                    return fence(t)
                tsk("pe", f_l1, reads=[tw1, tw2, T_xdfm], writes=[T_psA])

                def f_l1ev(a, og=og):
                    last = None
                    for ot in range(2):
                        last = a.activation(
                            out=s_sb[:, ot, :], in_=psA[:, ot * 512:ot * 512 + 512],
                            func=AF.Silu,
                            scale=sm["w1_s"][:, og * 2 + ot:og * 2 + ot + 1])
                    for ot in range(2):
                        last = a.activation(
                            out=t_sb[:, ot, :],
                            in_=psA[:, 1024 + ot * 512:1024 + ot * 512 + 512],
                            func=AF.Identity,
                            scale=sm["w2_s"][:, og * 2 + ot:og * 2 + ot + 1])
                    return last
                tsk("act", f_l1ev, reads=[T_psA, T_sm["w1_s"], T_sm["w2_s"]],
                    writes=[T_ssb, T_tsb])

                tsk("dve", lambda v, og=og: v.tensor_mul(
                    hbuf[:, og * 2:og * 2 + 2, :].rearrange("p a t -> p (a t)"),
                    s_sb[:].rearrange("p a t -> p (a t)"),
                    t_sb[:].rearrange("p a t -> p (a t)")),
                    reads=[T_ssb, T_tsb], writes=[T_hbuf])

            def tern_layer(wdram, scale_sm, src_buf, T_src, dst_buf, T_dst, dkpre):
                for og in range(16):
                    wbuf, tw = wb_e[og % 2], T_we[og % 2]
                    wv = wbuf[:, 0:8192].rearrange("p (k o) -> p k o", k=32)
                    tsk("sp", lambda g, og=og, wv=wv, wdram=wdram: g.dma_start(
                        out=wv, in_=wdram[og].rearrange("s p t -> p s t")),
                        writes=[tw], dma=True, dkey=f"{dkpre}_{og % 2}")

                    def f_mm(t, wv=wv):
                        last = None
                        for ot in range(2):
                            for k in range(32):
                                last = t.matmul(
                                    psA[:, ot * 512:ot * 512 + 512],
                                    wv[:, k, ot * 128:(ot + 1) * 128],
                                    src_buf[:, k, :], start=(k == 0),
                                    stop=(k == 31))
                        return fence(t)
                    tsk("pe", f_mm, reads=[tw, T_src], writes=[T_psA])

                    def f_ev(a, og=og):
                        last = None
                        for ot in range(2):
                            last = a.activation(
                                out=dst_buf[:, og * 2 + ot, :],
                                in_=psA[:, ot * 512:ot * 512 + 512], func=AF.Gelu,
                                scale=scale_sm[:, og * 2 + ot:og * 2 + ot + 1])
                        return last
                    tsk("act", f_ev, reads=[T_psA], writes=[T_dst])

            tern_layer(h1_t, sm["h1_s"], hbuf, T_hbuf, h2buf, T_h2buf, "th1")
            tern_layer(h2_t, sm["h2_s"], h2buf, T_h2buf, hbuf, T_hbuf, "th2")

            for og in range(4):
                wbuf, tw = wb_e[og % 2], T_we[og % 2]
                wv = wbuf[:, 0:8192].rearrange("p (k o) -> p k o", k=32)
                tsk("sp", lambda g, og=og, wv=wv: g.dma_start(
                    out=wv, in_=w3_t[og].rearrange("s p t -> p s t")),
                    writes=[tw], dma=True, dkey=f"tw3_{og % 2}")

                def f_w3(t, wv=wv):
                    last = None
                    for ot in range(2):
                        for k in range(32):
                            last = t.matmul(psA[:, ot * 512:ot * 512 + 512],
                                            wv[:, k, ot * 128:(ot + 1) * 128],
                                            hbuf[:, k, :], start=(k == 0),
                                            stop=(k == 31))
                    return fence(t)
                tsk("pe", f_w3, reads=[tw, T_hbuf], writes=[T_psA])

                def f_w3ev(a, og=og):
                    last = None
                    for ot in range(2):
                        last = a.activation(
                            out=f_fm[:, og * 2 + ot, :],
                            in_=psA[:, ot * 512:ot * 512 + 512], func=AF.Identity,
                            scale=sm["w3_s"][:, og * 2 + ot:og * 2 + ot + 1])
                    return last
                tsk("act", f_w3ev, reads=[T_psA, T_sm["w3_s"]], writes=[T_ffm],
                    extra=T_xdfm.wr() + T_ssb.wr() + T_tsb.wr() + T_xdtm.wr())

            if stage_upto < 5:
                continue

            scr.reset()
            T_ftmp, T_tmpD3 = T(), T()
            x2_evs = []
            for ti in range(NT):
                def f_ftr(t, ti=ti):
                    last = None
                    for ds in range(8):
                        last = t.transpose(psB[:, ds * 128:(ds + 1) * 128],
                                           f_fm[:, ds, ti * 128:(ti + 1) * 128],
                                           id_f32[:])
                    return fence(t)
                tsk("pe", f_ftr, reads=[T_ffm, T_id], writes=[T_psB])

                def f_fev(v, ti=ti):
                    return v.tensor_scalar(
                        out=ftmp, in0=psB[:, 0:1024], scalar1=1.0, scalar2=0.0,
                        op0=ALU.mult, op1=ALU.add, accum_out=statw[:, ti, 15:16])
                tsk("dve", f_fev, reads=[T_psB], writes=[T_ftmp, T_statw])

                ln_stats(ti, 15, ftmp, True, tmpD, T_tmpD3)
                sw = statw[:, ti, :]
                tsk("act", lambda a, ti=ti, sw=sw: a.activation(
                    out=tmpD, in_=ftmp, func=AF.Identity,
                    scale=sw[:, 18:19], bias=sw[:, 19:20]),
                    reads=[T_ftmp, T_statw], writes=[T_tmpD])

                def f_x2(v, ti=ti):
                    return v.scalar_tensor_tensor(
                        out=xwork[:, ti, :], in0=tmpD, scalar=1.0,
                        in1=xwork[:, ti, :], op0=ALU.mult, op1=ALU.add)
                ev = tsk("dve", f_x2, reads=[T_tmpD], writes=[T_xwork[ti]])
                x2_evs.append(ev)
                dbg(f"x2_{slab}_{ti}", xwork[:, ti, :], [T_xwork[ti]], [128, 1024])

            for ti in range(NT):
                ev = tsk("sp", lambda s, ti=ti, r=tok0 + ti * 128: s.dma_start(
                    out=x_out[r:r + 128, :], in_=xwork[:, ti, :]),
                    reads=[T_xwork[ti]], extra=x2_evs,
                    dma=True, dkey=f"xo{ti}")
                out_evs.append(ev)

        p.waits("sp", out_evs)

        p.fences = {
            "act": lambda a: a.copy(out=fscrap[:, 0:512],
                                    in_=fscrap[:, 0:512]),
            "dve": lambda v: v.tensor_copy(out=fscrap[:, 0:512],
                                           in_=fscrap[:, 0:512]),
        }
        sems = {s: e(nc.semaphore(s)) for s in
                list(MSEM.values()) + p.dma_sems}
        mk = p.replayer(sems)
        with nc.Block() as block:
            block.sync(lambda eng: mk("sp", eng))
            block.vector(lambda eng: mk("dve", eng))
            block.scalar(lambda eng: mk("act", eng))
            block.gpsimd(lambda eng: mk("pool", eng))
            block.tensor(lambda eng: mk("pe", eng))

    return nc


def make_in_maps(inp):
    tiles = prep_inputs(inp)
    x = np.ascontiguousarray(np.asarray(inp["L0_attn_x"], dtype=np.float32))
    in_maps = []
    for core in range(NCORE):
        m = dict(tiles)
        m["x"] = np.ascontiguousarray(
            x[core * NSEQ:(core + 1) * NSEQ].reshape(NSEQ * L, D))
        in_maps.append(m)
    return in_maps


def kernel(**inp):
    nc = build_program()
    in_maps = make_in_maps(inp)
    res = run_bass_kernel_spmd(nc, in_maps, core_ids=list(range(NCORE)))
    out = np.concatenate([r["out"].reshape(NSEQ, L, D) for r in res.results], axis=0)
    return out.astype(np.float32)


# version 04
# speedup vs baseline: 1.1222x; 1.1222x over previous
import contextlib
import numpy as np
import ml_dtypes
import concourse.bass as bass
import concourse.mybir as mybir
from concourse.bass_utils import run_bass_kernel_spmd

F32 = mybir.dt.float32
F32R = mybir.dt.float32r
BF16 = mybir.dt.bfloat16
AF = mybir.ActivationFunctionType
ALU = mybir.AluOpType
AX = mybir.AxisListType

D = 1024
DI = 2048
NS = 16
RK = 64
FFH = 8192
FFA = 4096
TH = 4096
L = 2048
NSEQ = 4
SLAB = 512
TC = 128
NCHUNK = SLAB // TC
NT = SLAB // 128
EPS = 1e-5
NCORE = 8

ENGINES = ("pe", "act", "dve", "pool", "sp")
MSEM = {"pe": "m_pe", "act": "m_act", "dve": "m_dve", "pool": "m_pool", "sp": "m_sp"}
DSEM = {"sp": "d_sp", "pool": "d_g"}
DTSIZE = {F32: 4, F32R: 4, BF16: 2}


def bc(ap, axis, n):
    pat = [list(x) for x in ap.ap]
    pat.insert(axis, [0, n])
    return bass.AP(ap.tensor, ap.offset, pat)


class Trk:
    __slots__ = ("w", "r")

    def __init__(self):
        self.w = None
        self.r = []

    def rd(self):
        return [self.w] if self.w else []

    def wr(self):
        return ([self.w] if self.w else []) + self.r

    def did_read(self, ev):
        self.r.append(ev)

    def did_write(self, ev):
        self.w = ev
        self.r = []


class Prog:
    def __init__(self):
        self.ops = {e: [] for e in ENGINES}
        self.cnt = {s: 0 for s in MSEM.values()}
        self.dma_sems = []

    def emit(self, eng, fn, deps=(), dma=False, dkey=None):
        if dma:
            sem = "d_" + (dkey or "misc")
            if sem not in self.cnt:
                self.cnt[sem] = 0
                self.dma_sems.append(sem)
            amt = 16
        else:
            sem = MSEM[eng]
            amt = 1
        self.cnt[sem] += amt
        ev = (sem, self.cnt[sem], amt)
        self.ops[eng].append((fn, [d for d in deps if d is not None], ev))
        return ev

    def waits(self, eng, deps):
        self.ops[eng].append((None, [d for d in deps if d is not None], None))

    def task(self, eng, fn, reads=(), writes=(), extra=(), dma=False, dkey=None):
        deps = list(extra)
        for b in reads:
            deps += b.rd()
        for b in writes:
            deps += b.wr()
        ev = self.emit(eng, fn, deps, dma=dma, dkey=dkey)
        for b in reads:
            b.did_read(ev)
        for b in writes:
            b.did_write(ev)
        return ev

    def __init_fences__(self):
        pass

    def replayer(self, sems):
        fences = getattr(self, "fences", {})
        def mk(name, engobj):
            waited = {}
            fence = fences.get(name)
            for fn, deps, inc in self.ops[name]:
                need = {}
                for (sem, val, _a) in deps:
                    if waited.get(sem, 0) >= val:
                        continue
                    need[sem] = max(need.get(sem, 0), val)
                for sem, val in sorted(need.items()):
                    engobj.wait_ge(sems[sem], val)
                    waited[sem] = val
                if fn is None:
                    continue
                instr = fn(engobj)
                if fence is not None:
                    instr = fence(engobj)
                sem, _val, amt = inc
                instr.then_inc(sems[sem], amt)
        return mk


class Scratch:

    def __init__(self, nc, name, nbytes, dtype=F32):
        self.cm = nc.sbuf_tensor("scr_" + name, [128, nbytes // 4], dtype)
        self.nbytes = nbytes
        self.off = 0
        self.events = []
        self.bar = []

    def __enter__(self):
        self.t = self.cm.__enter__()
        return self

    def __exit__(self, *a):
        return self.cm.__exit__(*a)

    def alloc(self, shape, dtype, parts=128):
        n = int(np.prod(shape)) * DTSIZE[dtype]
        n = (n + 15) // 16 * 16
        assert self.off + n <= self.nbytes, f"scratch overflow {self.off}+{n}"
        w0 = self.off // 4
        self.off += n
        flat = self.t[0:parts, w0:w0 + (n // 4)].bitcast(dtype)
        if len(shape) == 1:
            ap = flat[:, 0:shape[0]]
        elif len(shape) == 2:
            ap = flat[:, 0:shape[0] * shape[1]].rearrange(
                "p (a b) -> p a b", a=shape[0])
        else:
            ap = flat[:, 0:shape[0] * shape[1] * shape[2]].rearrange(
                "p (a b c) -> p a b c", a=shape[0], b=shape[1])
        return ap

    def reset(self):
        self.bar = self.bar + self.events
        self.events = []


def scr_task(p, regions, eng, fn, reads=(), writes=(), extra=(), dma=False,
             dkey=None):
    bar = []
    for r in regions:
        bar += r.bar
    ev = p.task(eng, fn, reads=reads, writes=writes,
                extra=list(extra) + bar, dma=dma, dkey=dkey)
    for r in regions:
        r.events.append(ev)
    return ev



def _tile_k_o(w, kblk, oblk, dtype=None):
    K, O = w.shape
    t = w.reshape(K // kblk, kblk, O // oblk, oblk).transpose(2, 0, 1, 3)
    if dtype is not None:
        t = t.astype(dtype)
    return np.ascontiguousarray(t)


def _perpart(v, nsl):
    return np.ascontiguousarray(v.reshape(nsl, 128).T.astype(np.float32))


def prep_inputs(inp):
    bf = ml_dtypes.bfloat16
    f32 = np.float32
    g = lambda k: np.asarray(inp[k], dtype=f32)
    d = {}
    d["w_in_t"] = _tile_k_o(g("L0_attn_w_in"), 128, 512)
    wx = g("L0_attn_w_x")
    wxp = np.zeros((DI, 128), f32)
    wxp[:, 0:16] = wx[:, RK:RK + NS]
    wxp[:, 32:48] = wx[:, RK + NS:RK + 2 * NS]
    wxp[:, 64:128] = wx[:, 0:RK]
    d["w_x_t"] = np.ascontiguousarray(wxp.reshape(16, 128, 128))
    d["w_dt_t"] = np.ascontiguousarray(g("L0_attn_w_dt"))
    d["w_out_t"] = _tile_k_o(g("L0_attn_w_out"), 128, 256)
    wf = g("L0_attn_w_ff1").reshape(8, 128, 32, 256)
    d["w_ff1_t"] = np.ascontiguousarray(
        np.stack([wf[:, :, 0:16], wf[:, :, 16:32]], axis=3)
        .transpose(2, 0, 3, 1, 4))
    d["w_ff2_t"] = _tile_k_o(g("L0_attn_w_ff2"), 128, 128)
    d["w1_t"] = _tile_k_o(np.ascontiguousarray(g("L0_ffn_w1_w").T), 128, 256, bf)
    d["w2_t"] = _tile_k_o(np.ascontiguousarray(g("L0_ffn_w2_w").T), 128, 256, bf)
    d["h1_t"] = _tile_k_o(np.ascontiguousarray(g("L0_ffn_h1_w").T), 128, 256, bf)
    d["h2_t"] = _tile_k_o(np.ascontiguousarray(g("L0_ffn_h2_w").T), 128, 256, bf)
    d["w3_t"] = _tile_k_o(np.ascontiguousarray(g("L0_ffn_w3_w").T), 128, 256, bf)
    d["ln1_g"] = _perpart(g("L0_attn_ln1_g"), 8)
    d["ln1_b"] = _perpart(g("L0_attn_ln1_b"), 8)
    d["ln2_g"] = _perpart(g("L0_attn_ln2_g"), 8)
    d["ln2_b"] = _perpart(g("L0_attn_ln2_b"), 8)
    d["conv_w"] = np.ascontiguousarray(
        g("L0_attn_conv_w").reshape(16, 128, 4).transpose(1, 0, 2))
    d["conv_b"] = _perpart(g("L0_attn_conv_b"), 16)
    d["b_dt"] = _perpart(g("L0_attn_b_dt"), 16)
    d["d_skip"] = _perpart(g("L0_attn_d_skip"), 16)
    d["a_log"] = np.ascontiguousarray(
        g("L0_attn_a_log").reshape(16, 128, NS).transpose(1, 0, 2))
    d["b_ff1"] = _perpart(g("L0_attn_b_ff1"), 64)
    d["b_ff2"] = _perpart(g("L0_attn_b_ff2"), 8)
    d["w1_s"] = _perpart(g("L0_ffn_w1_s"), 32)
    d["w2_s"] = _perpart(g("L0_ffn_w2_s"), 32)
    d["h1_s"] = _perpart(g("L0_ffn_h1_s"), 32)
    d["h2_s"] = _perpart(g("L0_ffn_h2_s"), 32)
    d["w3_s"] = _perpart(g("L0_ffn_w3_s"), 8)
    return d


SMALL_SHAPES = [("ln1_g", [128, 8]), ("ln1_b", [128, 8]), ("ln2_g", [128, 8]),
                ("ln2_b", [128, 8]), ("conv_w", [128, 16, 4]), ("conv_b", [128, 16]),
                ("b_dt", [128, 16]), ("d_skip", [128, 16]), ("a_log", [128, 16, NS]),
                ("b_ff1", [128, 64]), ("b_ff2", [128, 8]), ("w1_s", [128, 32]),
                ("w2_s", [128, 32]), ("h1_s", [128, 32]), ("h2_s", [128, 32]),
                ("w3_s", [128, 8])]



def build_program(n_slab=NSEQ * L // SLAB, stage_upto=99, debug_dump=()):
    nc = bass.Bass("TRN2", target_bir_lowering=False)
    dbg_names = set(debug_dump)

    x_in = nc.dram_tensor("x", [NSEQ * L, D], F32, kind="ExternalInput").ap()
    x_out = nc.dram_tensor("out", [NSEQ * L, D], F32, kind="ExternalOutput").ap()
    w_in_t = nc.dram_tensor("w_in_t", [8, 8, 128, 512], F32, kind="ExternalInput").ap()
    w_x_t = nc.dram_tensor("w_x_t", [16, 128, 128], F32, kind="ExternalInput").ap()
    w_dt_t = nc.dram_tensor("w_dt_t", [64, 2048], F32, kind="ExternalInput").ap()
    w_out_t = nc.dram_tensor("w_out_t", [4, 16, 128, 256], F32, kind="ExternalInput").ap()
    w_ff1_t = nc.dram_tensor("w_ff1_t", [16, 8, 2, 128, 256], F32, kind="ExternalInput").ap()
    w_ff2_t = nc.dram_tensor("w_ff2_t", [8, 32, 128, 128], F32, kind="ExternalInput").ap()
    w1_t = nc.dram_tensor("w1_t", [16, 8, 128, 256], BF16, kind="ExternalInput").ap()
    w2_t = nc.dram_tensor("w2_t", [16, 8, 128, 256], BF16, kind="ExternalInput").ap()
    h1_t = nc.dram_tensor("h1_t", [16, 32, 128, 256], BF16, kind="ExternalInput").ap()
    h2_t = nc.dram_tensor("h2_t", [16, 32, 128, 256], BF16, kind="ExternalInput").ap()
    w3_t = nc.dram_tensor("w3_t", [4, 32, 128, 256], BF16, kind="ExternalInput").ap()
    smalls = {nm: nc.dram_tensor(nm, shp, F32, kind="ExternalInput").ap()
              for nm, shp in SMALL_SHAPES}

    with contextlib.ExitStack() as ctx:
        e = ctx.enter_context
        sb = lambda nm, shape, dt_: e(nc.sbuf_tensor(nm, shape, dt_))

        sm = {nm: sb("s_" + nm, shp, F32) for nm, shp in SMALL_SHAPES}
        A_t = sb("A_t", [128, 16, NS], F32)
        iop = sb("iop", [128, 1], F32)
        cst = sb("cst", [128, 2], F32)
        sel_bf = sb("sel_bf", [16, 2048], BF16)
        id_f32 = sb("id_f32", [128, 128], F32)
        id_f32r = sb("id_f32r", [128, 128], F32R)
        id_bf16 = sb("id_bf16", [128, 128], BF16)
        fscrap = sb("fscrap", [128, 512], F32)
        xwork = sb("xwork", [128, NT, 1024], F32)
        stat_all = sb("stat", [128, NCHUNK, 12], F32)
        statw = sb("statw", [128, NT, 24], F32)
        h_state = sb("h_state", [128, 16, NS], F32)
        halo = sb("halo", [128, 16, 3], F32)
        scr = e(Scratch(nc, "main", 128 * 1024))
        scrr = e(Scratch(nc, "f32r", 40 * 1024, F32R))

        psA = e(nc.psum_tensor("psA", [128, 2048], F32))
        psB = e(nc.psum_tensor("psB", [128, 1024], F32))
        psC = e(nc.psum_tensor("psC", [128, 512], F32))
        psD = e(nc.psum_tensor("psD", [128, 512], F32))

        p = Prog()

        def fence(t):
            return t.matmul(psC[0:1, 256:384], id_bf16[:, 0:1], id_bf16[:],
                            start=True, stop=True)

        T = Trk
        T_sm = {nm: T() for nm in sm}
        T_statc = [T() for _ in range(NCHUNK)]
        T_A, T_sel, T_id, T_iop = T(), T(), T(), T()
        T_xwork = [T() for _ in range(NT)]
        T_statw, T_hst, T_halo = T(), T(), T()
        T_psA, T_psB, T_psC, T_psD = T(), T(), T(), T()
        T_xout = T()

        out_evs = []
        dbg_handles = {}

        def dbg(nm, src_ap, reads, shape, dt_=F32):
            if nm not in dbg_names:
                return
            if src_ap.dtype == F32R:
                src_ap = src_ap.bitcast(F32)
            t = nc.dram_tensor("dbg_" + nm, shape, dt_, kind="ExternalOutput").ap()
            dbg_handles[nm] = t
            ev = p.task("sp", lambda s, a=src_ap, o=t: s.dma_start(out=o, in_=a),
                        reads=reads, dma=True, dkey="dbg")
            out_evs.append(ev)

        sm_ev = None
        for nm, ap in smalls.items():
            sm_ev = p.task("sp", lambda s, ap=ap, nm=nm: s.dma_start(
                out=sm[nm][:], in_=ap), writes=[T_sm[nm]], dma=True, dkey="sm")
        for nm in sm:
            T_sm[nm].w = sm_ev

        io_scr = scr.alloc([2304], F32)

        def f_iota(g):
            g.iota(io_scr[0:128, 0:128], [[1, 128]],
                   channel_multiplier=0, allow_small_or_imprecise_dtypes=True)
            g.iota(iop[:], [[0, 1]], channel_multiplier=1,
                   allow_small_or_imprecise_dtypes=True)
            return g.iota(io_scr[0:16, 128:2176], [[1, 16], [0, 128]],
                          channel_multiplier=0, allow_small_or_imprecise_dtypes=True)
        ev_io = p.emit("pool", f_iota)
        T_iop.did_write(ev_io)

        def f_ids(g):
            g.tensor_scalar(out=id_f32[:], in0=io_scr[0:128, 0:128],
                            scalar1=iop[:], scalar2=None, op0=ALU.is_equal)
            g.tensor_scalar(out=id_f32r[:], in0=io_scr[0:128, 0:128], scalar1=iop[:],
                            scalar2=None, op0=ALU.is_equal)
            g.tensor_scalar(out=id_bf16[:], in0=io_scr[0:128, 0:128], scalar1=iop[:],
                            scalar2=None, op0=ALU.is_equal)
            return g.tensor_scalar(out=sel_bf[:], in0=io_scr[0:16, 128:2176],
                                   scalar1=iop[0:16, :], scalar2=None, op0=ALU.is_equal)
        ev_ids = p.emit("pool", f_ids, deps=[ev_io])
        T_id.did_write(ev_ids)
        T_sel.did_write(ev_ids)
        scr.events.append(ev_io)
        scr.events.append(ev_ids)

        T_cst = T()
        p.task("dve", lambda v: v.memset(cst[:, 0:1], EPS), writes=[T_cst])
        p.task("dve", lambda v: v.memset(cst[:, 1:2], 1.0), writes=[T_cst])
        p.task("act", lambda act: act.activation(
            out=A_t[:].rearrange("p a n -> p (a n)"),
            in_=sm["a_log"][:].rearrange("p a n -> p (a n)"), func=AF.Exp),
            reads=[T_sm["a_log"]], writes=[T_A])
        p.task("dve", lambda v: v.tensor_scalar_mul(
            A_t[:].rearrange("p a n -> p (a n)"),
            A_t[:].rearrange("p a n -> p (a n)"), -1.0),
            reads=[T_A], writes=[T_A])

        scr.off = 0
        scrr.off = 0
        x0c = scr.alloc([NT, 1024], F32)
        tmpA = scr.alloc([1024], F32)
        h_tm = scrr.alloc([1024], F32R)
        h_fm = scrr.alloc([8, TC], F32R)
        xc_raw = scr.alloc([16, 3 + TC], F32)
        xc_act = scrr.alloc([16, TC], F32R)
        z_bf = scr.alloc([16, TC], BF16)
        dt_f = scr.alloc([16, TC], F32)
        sp_t1 = scr.alloc([16, TC], F32)
        dtlo_r = scrr.alloc([TC], F32R, parts=64)
        b_sb = scr.alloc([TC], BF16, parts=16)
        c_sb = scr.alloc([TC], BF16, parts=16)
        dA = [scr.alloc([NS, TC], F32) for _ in range(2)]
        u_bf = [scr.alloc([NS, TC], BF16) for _ in range(2)]
        hall = [scr.alloc([NS, TC], BF16) for _ in range(2)]
        brep = scr.alloc([NS, TC], BF16)
        crep = scr.alloc([NS, TC], BF16)
        dtx = [scr.alloc([TC], F32) for _ in range(2)]
        ytree = [scr.alloc([NS, TC], BF16) for _ in range(2)]
        ygate = scr.alloc([16, TC], BF16)
        gtmp = [scr.alloc([TC], F32) for _ in range(2)]
        m_fm = scr.alloc([8, TC], F32)
        wb_in = [scrr.alloc([512], F32R) for _ in range(2)]
        wb_wx = scrr.alloc([16, 128], F32R)
        wb_wdt = scrr.alloc([2048], F32R, parts=64)
        wb_out = [scr.alloc([16, 256], BF16) for _ in range(2)]
        scrB_end = scr.off
        scrrB_end = scrr.off

        scr.off = 0
        h2tm = scr.alloc([1024], BF16)
        h2fm = scr.alloc([8, SLAB], BF16)
        wb_ff1 = [scr.alloc([8, 2, 256], BF16) for _ in range(2)]
        a_sb = scr.alloc([2, SLAB], BF16)
        g_sb = scr.alloc([2, SLAB], BF16)
        h2b = scr.alloc([32, SLAB], BF16)
        wb_ff2 = [scr.alloc([32, 128], BF16) for _ in range(2)]
        gg_fm = scr.alloc([8, SLAB], F32)
        scrC_end = scr.off

        scr.off = 0
        hbuf = scr.alloc([32, SLAB], BF16)
        h2buf = scr.alloc([32, SLAB], BF16)
        wb_e = [scr.alloc([8192], BF16) for _ in range(2)]
        offE = scr.off
        xd_fm = scr.alloc([8, SLAB], BF16)
        s_sb = scr.alloc([2, SLAB], BF16)
        t_sb = scr.alloc([2, SLAB], BF16)
        xdtm = scr.alloc([1024], BF16)
        endE1 = scr.off
        scr.off = offE
        f_fm = scr.alloc([8, SLAB], F32)
        scrE_end = max(scr.off, endE1)
        assert scrC_end <= 112 * 1024, scrC_end
        assert scrE_end <= 112 * 1024, scrE_end

        scr.off = 112 * 1024
        x0c2 = scr.alloc([1024], F32)
        tmpD = scr.alloc([1024], F32)
        ftmp = scr.alloc([1024], F32)
        scrD_end = scr.off
        assert scrD_end <= 128 * 1024, scrD_end

        for slab in range(n_slab):
            tok0 = (slab // (L // SLAB)) * L + (slab % (L // SLAB)) * SLAB
            seq_start = (slab % (L // SLAB) == 0)

            scr.reset()
            scrr.reset()

            T_x0c, T_tmpA, T_htm, T_hfm = T(), T(), T(), T()
            T_xcraw, T_xcact, T_zbf = T(), T(), T()
            T_dt, T_sp1, T_dtlo, T_bsb, T_csb = T(), T(), T(), T(), T()
            T_dA = [T(), T()]
            T_u = [T(), T()]
            T_hall = [T(), T()]
            T_brep, T_crep = T(), T()
            T_dtx = [T(), T()]
            T_ytree = [T(), T()]
            T_gtmp = [T(), T()]
            T_ygate, T_mfm = T(), T()
            T_win = [T(), T()]
            T_wx, T_wdt = T(), T()
            T_wout = [T(), T()]
            tsk = lambda *a, **k: scr_task(p, (scr, scrr), *a, **k)

            tsk("pool", lambda g: g.dma_start(
                out=wb_wx, in_=w_x_t.rearrange("s p c -> p s c")),
                writes=[T_wx], dma=True, dkey="wx")
            tsk("pool", lambda g: g.dma_start(out=wb_wdt, in_=w_dt_t),
                writes=[T_wdt], dma=True, dkey="wdt")
            tsk("sp", lambda s, r=tok0: s.dma_start(
                out=x0c,
                in_=x_in[r:r + SLAB, :].rearrange("(a p) d -> p a d", p=128)),
                writes=[T_x0c], dma=True, dkey="x0")

            dbg(f"x0c_{slab}", x0c.rearrange("p a d -> p (a d)"), [T_x0c],
                [128, NT * 1024])
            if seq_start:
                p.task("dve", lambda v: v.memset(
                    halo[:].rearrange("p s t -> p (s t)"), 0.0), writes=[T_halo])
                p.task("dve", lambda v: v.memset(
                    h_state[:].rearrange("p s n -> p (s n)"), 0.0), writes=[T_hst])

            for c in range(NCHUNK):
                stat = stat_all[:, c, :]
                T_stat = T_statc[c]
                def f_st0(v, c=c, stat=stat):
                    return v.tensor_reduce(out=stat[:, 0:1], in_=x0c[:, c, :],
                                           axis=AX.X, op=ALU.add)
                tsk("dve", f_st0, reads=[T_x0c], writes=[T_stat])

                def f_st0b(v, stat=stat):
                    return v.tensor_scalar(out=stat[:, 1:2], in0=stat[:, 0:1],
                                           scalar1=1.0 / D, scalar2=None, op0=ALU.mult)
                tsk("dve", f_st0b, reads=[T_stat], writes=[T_stat])
                dbg(f"stA_{slab}_{c}", stat, [T_stat], [128, 12])

                def f_st1(act, c=c, stat=stat):
                    return act.activation(out=tmpA, in_=x0c[:, c, :], func=AF.Square,
                                          scale=-1.0, bias=stat[:, 1:2],
                                          accum_out=stat[:, 2:3])
                tsk("act", f_st1, reads=[T_x0c, T_stat], writes=[T_tmpA, T_stat])
                dbg(f"stB_{slab}_{c}", stat, [T_stat], [128, 12])

                def f_st2(act, stat=stat):
                    act.activation(out=stat[:, 3:4], in_=stat[:, 2:3], func=AF.Ln,
                                   scale=1.0 / D, bias=cst[:, 0:1])
                    act.activation(out=stat[:, 4:5], in_=stat[:, 3:4], func=AF.Exp,
                                   scale=-0.5)
                    return act.activation(out=stat[:, 5:6], in_=stat[:, 3:4],
                                          func=AF.Exp)
                tsk("act", f_st2, reads=[T_stat, T_cst], writes=[T_stat])
                dbg(f"stC_{slab}_{c}", stat, [T_stat], [128, 12])

                def f_st3(v, stat=stat):
                    return v.reciprocal(out=stat[:, 11:12], in_=stat[:, 5:6])
                tsk("dve", f_st3, reads=[T_stat], writes=[T_stat])

                def f_st3b(v, stat=stat):
                    return v.tensor_scalar(out=stat[:, 6:7], in0=stat[:, 2:3],
                                           scalar1=1.0 / D, scalar2=stat[:, 11:12],
                                           op0=ALU.mult, op1=ALU.mult)
                tsk("dve", f_st3b, reads=[T_stat], writes=[T_stat])

                def f_st4(act, stat=stat):
                    act.activation(out=stat[:, 7:8], in_=stat[:, 6:7], func=AF.Ln,
                                   bias=cst[:, 0:1])
                    return act.activation(out=stat[:, 7:8], in_=stat[:, 7:8],
                                          func=AF.Exp, scale=-0.5)
                tsk("act", f_st4, reads=[T_stat, T_cst], writes=[T_stat])

                def f_st5(v, stat=stat):
                    v.tensor_scalar(out=stat[:, 8:9], in0=stat[:, 4:5],
                                    scalar1=stat[:, 7:8], scalar2=None, op0=ALU.mult)
                    return v.tensor_scalar(out=stat[:, 10:11], in0=stat[:, 1:2],
                                           scalar1=stat[:, 4:5], scalar2=-1.0,
                                           op0=ALU.mult, op1=ALU.mult)
                tsk("dve", f_st5, reads=[T_stat], writes=[T_stat])

                def f_st5b(v, stat=stat):
                    return v.tensor_scalar(out=stat[:, 9:10], in0=stat[:, 1:2],
                                           scalar1=stat[:, 8:9], scalar2=-1.0,
                                           op0=ALU.mult, op1=ALU.mult)
                tsk("dve", f_st5b, reads=[T_stat], writes=[T_stat])

                def f_htm(act, c=c, stat=stat):
                    return act.activation(out=h_tm, in_=x0c[:, c, :], func=AF.Identity,
                                          scale=stat[:, 8:9], bias=stat[:, 9:10])
                dbg(f"stat_{slab}_{c}", stat, [T_stat], [128, 12])
                tsk("act", f_htm, reads=[T_x0c, T_stat], writes=[T_htm])
                dbg(f"htm_{slab}_{c}", h_tm, [T_htm], [128, 1024])

                def f_htr(t):
                    last = None
                    for ds in range(8):
                        last = t.transpose(
                            psA[:, ds * 256:ds * 256 + 128].bitcast(F32R),
                            h_tm[:, ds * 128:(ds + 1) * 128], id_f32r[:])
                    return fence(t)
                tsk("pe", f_htr, reads=[T_htm, T_id], writes=[T_psA])

                def f_hev(act):
                    last = None
                    for ds in range(8):
                        last = act.activation(
                            out=h_fm[:, ds, :],
                            in_=psA[:, ds * 256:ds * 256 + 128].bitcast(F32R),
                            func=AF.Identity, scale=sm["ln1_g"][:, ds:ds + 1],
                            bias=sm["ln1_b"][:, ds:ds + 1])
                    return last
                tsk("act", f_hev, reads=[T_psA, T_sm["ln1_g"], T_sm["ln1_b"]],
                    writes=[T_hfm])
                dbg(f"hfm_{slab}_{c}", h_fm.rearrange("p a t -> p (a t)"),
                    [T_hfm], [128, 8 * TC])

                for o8 in range(8):
                    for k in range(8):
                        wbuf, tw = wb_in[k % 2], T_win[k % 2]
                        tsk("pool", lambda g, o8=o8, k=k, wbuf=wbuf: g.dma_start(
                            out=wbuf, in_=w_in_t[o8, k]), writes=[tw], dma=True,
                            dkey=f"win{k % 2}")

                        def f_mm(t, k=k, wbuf=wbuf):
                            last = None
                            for ot in range(4):
                                last = t.matmul(
                                    psA[:, ot * 512:ot * 512 + 128],
                                    wbuf[:, ot * 128:(ot + 1) * 128],
                                    h_fm[:, k, :], start=(k == 0), stop=(k == 7))
                            if k == 7:
                                last = fence(t)
                            return last
                        tsk("pe", f_mm, reads=[tw, T_hfm],
                            writes=([T_psA] if k == 7 else []),
                            extra=(T_psA.wr() if k == 0 else []))
                    if o8 < 4:
                        def f_evq(act, o8=o8):
                            last = None
                            for ot in range(4):
                                last = act.activation(
                                    out=xc_raw[:, o8 * 4 + ot, 3:3 + TC],
                                    in_=psA[:, ot * 512:ot * 512 + 128], func=AF.Copy)
                            return last
                        tsk("act", f_evq, reads=[T_psA], writes=[T_xcraw])
                    else:
                        def f_evz(act, o8=o8):
                            last = None
                            for ot in range(4):
                                last = act.activation(
                                    out=z_bf[:, (o8 - 4) * 4 + ot, :],
                                    in_=psA[:, ot * 512:ot * 512 + 128], func=AF.Silu)
                            return last
                        tsk("act", f_evz, reads=[T_psA], writes=[T_zbf])

                dbg(f"xcraw_{slab}_{c}",
                    xc_raw[:, :, 3:3 + TC].rearrange("p s t -> p s t"),
                    [T_xcraw], [128, 16, TC])

                tsk("dve", lambda v: v.tensor_copy(
                    out=xc_raw[:, :, 0:3], in_=halo[:]),
                    reads=[T_halo], writes=[T_xcraw])

                def f_conv(v):
                    last = None
                    for s16 in range(16):
                        v.tensor_scalar(out=sp_t1[:, s16, :], in0=xc_raw[:, s16, 3:3 + TC],
                                        scalar1=sm["conv_w"][:, s16, 3:4], scalar2=None,
                                        op0=ALU.mult)
                        for j in range(3):
                            last = v.scalar_tensor_tensor(
                                out=sp_t1[:, s16, :], in0=xc_raw[:, s16, j:j + TC],
                                scalar=sm["conv_w"][:, s16, j:j + 1],
                                in1=sp_t1[:, s16, :], op0=ALU.mult, op1=ALU.add)
                    return last
                tsk("dve", f_conv, reads=[T_xcraw, T_sm["conv_w"]], writes=[T_sp1])

                tsk("dve", lambda v: v.tensor_copy(
                    out=halo[:], in_=xc_raw[:, :, TC:TC + 3]),
                    reads=[T_xcraw], writes=[T_halo])

                def f_silu(act):
                    last = None
                    for s16 in range(16):
                        last = act.activation(out=xc_act[:, s16, :], in_=sp_t1[:, s16, :],
                                              func=AF.Silu,
                                              bias=sm["conv_b"][:, s16:s16 + 1])
                    return last
                tsk("act", f_silu, reads=[T_sp1, T_sm["conv_b"]], writes=[T_xcact])
                dbg(f"xcact_{slab}_{c}", xc_act.rearrange("p s t -> p (s t)"),
                    [T_xcact], [128, 16 * TC])

                def f_dbc(t):
                    last = None
                    for k in range(16):
                        last = t.matmul(psC[:, 0:TC], wb_wx[:, k, :], xc_act[:, k, :],
                                        start=(k == 0), stop=(k == 15))
                    return fence(t)
                tsk("pe", f_dbc, reads=[T_wx, T_xcact], writes=[T_psC])

                def f_dbc_ev(act):
                    act.activation(out=b_sb, in_=psC[0:16, 0:TC], func=AF.Copy)
                    act.activation(out=c_sb, in_=psC[32:48, 0:TC], func=AF.Copy)
                    return act.activation(out=dtlo_r, in_=psC[64:128, 0:TC],
                                          func=AF.Copy)
                tsk("act", f_dbc_ev, reads=[T_psC], writes=[T_bsb, T_csb, T_dtlo])

                for grp in range(2):
                    def f_wdt(t, grp=grp):
                        last = None
                        for j in range(8):
                            ds = grp * 8 + j
                            last = t.matmul(psB[:, j * 128:(j + 1) * 128],
                                            wb_wdt[:, ds * 128:(ds + 1) * 128],
                                            dtlo_r, start=True, stop=True)
                        return fence(t)
                    tsk("pe", f_wdt, reads=[T_wdt, T_dtlo], writes=[T_psB])

                    def f_sp(act, grp=grp):
                        last = None
                        for j in range(8):
                            ds = grp * 8 + j
                            pb = psB[:, j * 128:(j + 1) * 128]
                            act.activation(out=sp_t1[:, ds, :], in_=pb, func=AF.Abs,
                                           bias=sm["b_dt"][:, ds:ds + 1])
                            act.activation(out=dt_f[:, ds, :], in_=pb, func=AF.Relu,
                                           bias=sm["b_dt"][:, ds:ds + 1])
                            act.activation(out=sp_t1[:, ds, :], in_=sp_t1[:, ds, :],
                                           func=AF.Exp, scale=-1.0)
                            last = act.activation(out=sp_t1[:, ds, :],
                                                  in_=sp_t1[:, ds, :], func=AF.Ln,
                                                  bias=cst[:, 1:2])
                        return last
                    tsk("act", f_sp, reads=[T_psB, T_sm["b_dt"], T_cst],
                        writes=[T_sp1, T_dt])

                tsk("dve", lambda v: v.tensor_add(
                    dt_f.rearrange("p s t -> p (s t)"),
                    dt_f.rearrange("p s t -> p (s t)"),
                    sp_t1.rearrange("p s t -> p (s t)")),
                    reads=[T_sp1, T_dt], writes=[T_dt])
                dbg(f"dt_{slab}_{c}", dt_f.rearrange("p s t -> p (s t)"),
                    [T_dt], [128, 16 * TC])

                def f_bcB(t):
                    last = None
                    for n in range(NS):
                        last = t.matmul(psA[:, n * TC:(n + 1) * TC],
                                        sel_bf[:, n * 128:(n + 1) * 128], b_sb,
                                        start=True, stop=True)
                    return fence(t)
                tsk("pe", f_bcB, reads=[T_sel, T_bsb], writes=[T_psA])
                tsk("act", lambda act: act.activation(
                    out=brep.rearrange("p n t -> p (n t)"),
                    in_=psA[:, 0:2048], func=AF.Copy),
                    reads=[T_psA], writes=[T_brep])

                def f_bcC(t):
                    last = None
                    for n in range(NS):
                        last = t.matmul(psA[:, n * TC:(n + 1) * TC],
                                        sel_bf[:, n * 128:(n + 1) * 128], c_sb,
                                        start=True, stop=True)
                    return fence(t)
                tsk("pe", f_bcC, reads=[T_sel, T_csb], writes=[T_psA])
                tsk("act", lambda act: act.activation(
                    out=crep.rearrange("p n t -> p (n t)"),
                    in_=psA[:, 0:2048], func=AF.Copy),
                    reads=[T_psA], writes=[T_crep])

                for ds in range(16):
                    pr = ds % 2
                    dA_, u_, hall_ = dA[pr], u_bf[pr], hall[pr]
                    dtx_, ytree_, gtmp_ = dtx[pr], ytree[pr], gtmp[pr]
                    TdA, Tu, Thall = T_dA[pr], T_u[pr], T_hall[pr]
                    Tdtx, Tyt, Tg = T_dtx[pr], T_ytree[pr], T_gtmp[pr]

                    def f_e(v, ds=ds, dA_=dA_):
                        a_b = bc(A_t[:, ds, :], 2, TC)
                        d_b = bc(dt_f[:, ds, :], 1, NS)
                        return v.tensor_tensor(out=dA_[:], in0=a_b, in1=d_b,
                                               op=ALU.mult)
                    tsk("dve", f_e, reads=[T_A, T_dt], writes=[TdA])

                    tsk("act", lambda act, dA_=dA_: act.activation(
                        out=dA_.rearrange("p n t -> p (n t)"),
                        in_=dA_.rearrange("p n t -> p (n t)"), func=AF.Exp),
                        reads=[TdA], writes=[TdA])

                    def f_dtx(v, ds=ds, dtx_=dtx_):
                        return v.tensor_mul(dtx_, dt_f[:, ds, :],
                                            xc_act[:, ds, :].bitcast(F32))
                    tsk("dve", f_dtx, reads=[T_dt, T_xcact], writes=[Tdtx])

                    def f_u(v, u_=u_, dtx_=dtx_):
                        return v.tensor_tensor(out=u_[:], in0=brep[:],
                                               in1=bc(dtx_, 1, NS), op=ALU.mult)
                    tsk("dve", f_u, reads=[Tdtx, T_brep], writes=[Tu])

                    def f_scan(v, ds=ds, hall_=hall_, dA_=dA_, u_=u_):
                        last = None
                        for n in range(NS):
                            last = v.tensor_tensor_scan(
                                out=hall_[:, n, :], data0=dA_[:, n, :],
                                data1=u_[:, n, :],
                                initial=h_state[:, ds, n:n + 1],
                                op0=ALU.mult, op1=ALU.add)
                        return last
                    tsk("dve", f_scan, reads=[TdA, Tu, T_hst], writes=[Thall])

                    def f_hst(v, ds=ds, hall_=hall_):
                        return v.tensor_copy(out=h_state[:, ds, :],
                                             in_=hall_[:, :, TC - 1])
                    tsk("dve", f_hst, reads=[Thall], writes=[T_hst])

                    tsk("dve", lambda v, ytree_=ytree_, hall_=hall_: v.tensor_mul(
                        ytree_.rearrange("p n t -> p (n t)"),
                        hall_.rearrange("p n t -> p (n t)"),
                        crep.rearrange("p n t -> p (n t)")),
                        reads=[Thall, T_crep], writes=[Tyt])

                    def f_yt(v, ytree_=ytree_):
                        last = None
                        for hn in (8, 4, 2, 1):
                            last = v.tensor_add(
                                ytree_[:, 0:hn, :].rearrange("p n t -> p (n t)"),
                                ytree_[:, 0:hn, :].rearrange("p n t -> p (n t)"),
                                ytree_[:, hn:2 * hn, :].rearrange("p n t -> p (n t)"))
                        return last
                    tsk("dve", f_yt, reads=[Tyt], writes=[Tyt])

                    def f_g1(v, ds=ds, gtmp_=gtmp_, ytree_=ytree_):
                        return v.scalar_tensor_tensor(
                            out=gtmp_, in0=xc_act[:, ds, :].bitcast(F32),
                            scalar=sm["d_skip"][:, ds:ds + 1], in1=ytree_[:, 0, :],
                            op0=ALU.mult, op1=ALU.add)
                    tsk("dve", f_g1, reads=[T_xcact, T_sm["d_skip"], Tyt],
                        writes=[Tg])

                    def f_g2(v, ds=ds, gtmp_=gtmp_):
                        return v.tensor_mul(ygate[:, ds, :], gtmp_, z_bf[:, ds, :])
                    tsk("dve", f_g2, reads=[Tg, T_zbf], writes=[T_ygate])

                dbg(f"ygate_{slab}_{c}", ygate.rearrange("p s t -> p (s t)"),
                    [T_ygate], [128, 16 * TC], BF16)

                for oqr in range(4):
                    wbuf, tw = wb_out[oqr % 2], T_wout[oqr % 2]
                    tsk("pool", lambda g, oqr=oqr, wbuf=wbuf: g.dma_start(
                        out=wbuf, in_=w_out_t[oqr].rearrange("s p t -> p s t")),
                        writes=[tw], dma=True, dkey=f"wout{oqr % 2}")

                    def f_wout(t, wbuf=wbuf):
                        last = None
                        for hf in range(2):
                            for k in range(16):
                                last = t.matmul(psD[:, hf * 128:(hf + 1) * 128],
                                                wbuf[:, k, hf * 128:(hf + 1) * 128],
                                                ygate[:, k, :], start=(k == 0),
                                                stop=(k == 15))
                        return fence(t)
                    tsk("pe", f_wout, reads=[tw, T_ygate], writes=[T_psD])

                    def f_mev(act, oqr=oqr):
                        last = None
                        for hf in range(2):
                            last = act.activation(out=m_fm[:, oqr * 2 + hf, :],
                                                  in_=psD[:, hf * 128:(hf + 1) * 128],
                                                  func=AF.Copy)
                        return last
                    tsk("act", f_mev, reads=[T_psD], writes=[T_mfm])

                dbg(f"mfm_{slab}_{c}", m_fm.rearrange("p a t -> p (a t)"),
                    [T_mfm], [128, 8 * TC])

                def f_mtr(t):
                    last = None
                    for ds in range(8):
                        last = t.transpose(psB[:, ds * 128:(ds + 1) * 128],
                                           m_fm[:, ds, :], id_f32[:])
                    return fence(t)
                tsk("pe", f_mtr, reads=[T_mfm, T_id], writes=[T_psB])

                def f_xa(act, c=c, stat=stat):
                    return act.activation(out=tmpA, in_=x0c[:, c, :], func=AF.Identity,
                                          scale=stat[:, 4:5], bias=stat[:, 10:11])
                tsk("act", f_xa, reads=[T_x0c, T_stat], writes=[T_tmpA])

                def f_xb(v, c=c):
                    return v.scalar_tensor_tensor(
                        out=xwork[:, c, :], in0=psB[:, 0:1024], scalar=1.0,
                        in1=tmpA, op0=ALU.mult, op1=ALU.add,
                        accum_out=statw[:, c, 0:1])
                tsk("dve", f_xb, reads=[T_psB, T_tmpA],
                    writes=[T_xwork[c], T_statw])
                dbg(f"xb_{slab}_{c}", xwork[:, c, :], [T_xwork[c]], [128, 1024])

            if stage_upto < 2:
                continue

            def ln_stats(ti, base, src_tile, have_sum, tmp_out, T_tmp):
                sw = statw[:, ti, :]
                if not have_sum:
                    tsk("dve", lambda v, s=src_tile, sw=sw: v.tensor_reduce(
                        out=sw[:, base:base + 1], in_=s, axis=AX.X, op=ALU.add),
                        reads=[T_statw], writes=[T_statw])
                tsk("dve", lambda v, sw=sw: v.tensor_scalar(
                    out=sw[:, base + 2:base + 3], in0=sw[:, base:base + 1],
                    scalar1=1.0 / D, scalar2=None, op0=ALU.mult),
                    reads=[T_statw], writes=[T_statw])
                tsk("act", lambda a, s=src_tile, sw=sw: a.activation(
                    out=tmp_out, in_=s, func=AF.Square, scale=-1.0,
                    bias=sw[:, base + 2:base + 3],
                    accum_out=sw[:, base + 1:base + 2]),
                    reads=[T_statw], writes=[T_statw, T_tmp])
                def f_r(a, sw=sw):
                    a.activation(out=sw[:, base + 3:base + 4],
                                 in_=sw[:, base + 1:base + 2], func=AF.Ln,
                                 scale=1.0 / D, bias=cst[:, 0:1])
                    return a.activation(out=sw[:, base + 3:base + 4],
                                        in_=sw[:, base + 3:base + 4], func=AF.Exp,
                                        scale=-0.5)
                tsk("act", f_r, reads=[T_statw, T_cst], writes=[T_statw])
                tsk("dve", lambda v, sw=sw: v.tensor_scalar(
                    out=sw[:, base + 4:base + 5], in0=sw[:, base + 2:base + 3],
                    scalar1=sw[:, base + 3:base + 4], scalar2=-1.0,
                    op0=ALU.mult, op1=ALU.mult),
                    reads=[T_statw], writes=[T_statw])

            scr.reset()
            scrr.reset()
            T_h2tm, T_h2fm, T_asb, T_gsb, T_h2b, T_ggfm = T(), T(), T(), T(), T(), T()
            T_tmpD0 = T()
            T_wff1 = [T(), T()]
            T_wff2 = [T(), T()]

            for ti in range(NT):
                ln_stats(ti, 0, xwork[:, ti, :], True, tmpD, T_tmpD0)
                sw = statw[:, ti, :]
                tsk("act", lambda a, ti=ti, sw=sw: a.activation(
                    out=h2tm, in_=xwork[:, ti, :], func=AF.Identity,
                    scale=sw[:, 3:4], bias=sw[:, 4:5]),
                    reads=[T_xwork[ti], T_statw], writes=[T_h2tm])

                def f_h2tr(t):
                    pa = psA.bitcast(BF16)
                    last = None
                    for ds in range(8):
                        last = t.transpose(pa[:, ds * 128:(ds + 1) * 128],
                                           h2tm[:, ds * 128:(ds + 1) * 128],
                                           id_bf16[:])
                    return fence(t)
                tsk("pe", f_h2tr, reads=[T_h2tm, T_id], writes=[T_psA])

                def f_h2ev(a, ti=ti):
                    pa = psA.bitcast(BF16)
                    last = None
                    for ds in range(8):
                        last = a.activation(
                            out=h2fm[:, ds, ti * 128:(ti + 1) * 128],
                            in_=pa[:, ds * 128:(ds + 1) * 128], func=AF.Identity,
                            scale=sm["ln2_g"][:, ds:ds + 1],
                            bias=sm["ln2_b"][:, ds:ds + 1])
                    return last
                tsk("act", f_h2ev, reads=[T_psA, T_sm["ln2_g"], T_sm["ln2_b"]],
                    writes=[T_h2fm])

            for og in range(16):
                wbuf, tw = wb_ff1[og % 2], T_wff1[og % 2]
                tsk("pool", lambda g, og=og, wbuf=wbuf: g.dma_start(
                    out=wbuf, in_=w_ff1_t[og].rearrange("k h p t -> p k h t")),
                    writes=[tw], dma=True, dkey=f"ff1_{og % 2}")

                def f_ff1(t, wbuf=wbuf):
                    last = None
                    for half in range(2):
                        for ot in range(2):
                            for k in range(8):
                                last = t.matmul(
                                    psA[:, (half * 2 + ot) * 512:
                                        (half * 2 + ot) * 512 + 512],
                                    wbuf[:, k, half, ot * 128:(ot + 1) * 128],
                                    h2fm[:, k, :], start=(k == 0), stop=(k == 7))
                    return fence(t)
                tsk("pe", f_ff1, reads=[tw, T_h2fm], writes=[T_psA])

                def f_aev(a, og=og):
                    last = None
                    for ot in range(2):
                        last = a.activation(
                            out=a_sb[:, ot, :], in_=psA[:, ot * 512:ot * 512 + 512],
                            func=AF.Identity,
                            bias=sm["b_ff1"][:, og * 2 + ot:og * 2 + ot + 1])
                    for ot in range(2):
                        last = a.activation(
                            out=g_sb[:, ot, :],
                            in_=psA[:, 1024 + ot * 512:1024 + ot * 512 + 512],
                            func=AF.Gelu_apprx_tanh,
                            bias=sm["b_ff1"][:, 32 + og * 2 + ot:32 + og * 2 + ot + 1])
                    return last
                tsk("act", f_aev, reads=[T_psA, T_sm["b_ff1"]],
                    writes=[T_asb, T_gsb])

                def f_gmul(v, og=og):
                    return v.tensor_mul(
                        h2b[:, og * 2:og * 2 + 2, :].rearrange("p a t -> p (a t)"),
                        a_sb[:].rearrange("p a t -> p (a t)"),
                        g_sb[:].rearrange("p a t -> p (a t)"))
                tsk("dve", f_gmul, reads=[T_asb, T_gsb], writes=[T_h2b])

            for o2t in range(8):
                wbuf, tw = wb_ff2[o2t % 2], T_wff2[o2t % 2]
                tsk("pool", lambda g, o2t=o2t, wbuf=wbuf: g.dma_start(
                    out=wbuf, in_=w_ff2_t[o2t].rearrange("s p t -> p s t")),
                    writes=[tw], dma=True, dkey=f"ff2_{o2t % 2}")

                def f_ff2(t, wbuf=wbuf):
                    last = None
                    for k in range(32):
                        last = t.matmul(psB[:, 0:512], wbuf[:, k, :],
                                        h2b[:, k, :], start=(k == 0), stop=(k == 31))
                    return fence(t)
                tsk("pe", f_ff2, reads=[tw, T_h2b], writes=[T_psB])

                tsk("act", lambda a, o2t=o2t: a.activation(
                    out=gg_fm[:, o2t, :], in_=psB[:, 0:512], func=AF.Identity,
                    bias=sm["b_ff2"][:, o2t:o2t + 1]),
                    reads=[T_psB, T_sm["b_ff2"]], writes=[T_ggfm])

            for ti in range(NT):
                def f_ggtr(t, ti=ti):
                    last = None
                    for ds in range(8):
                        last = t.transpose(psB[:, ds * 128:(ds + 1) * 128],
                                           gg_fm[:, ds, ti * 128:(ti + 1) * 128],
                                           id_f32[:])
                    return fence(t)
                tsk("pe", f_ggtr, reads=[T_ggfm, T_id], writes=[T_psB])

                def f_xc2(v, ti=ti):
                    return v.scalar_tensor_tensor(
                        out=xwork[:, ti, :], in0=psB[:, 0:1024], scalar=1.0,
                        in1=xwork[:, ti, :], op0=ALU.mult, op1=ALU.add,
                        accum_out=statw[:, ti, 5:6])
                tsk("dve", f_xc2, reads=[T_psB], writes=[T_xwork[ti], T_statw])
                dbg(f"xc2_{slab}_{ti}", xwork[:, ti, :], [T_xwork[ti]], [128, 1024])

            if stage_upto < 3:
                continue

            scr.reset()
            T_x0c2, T_tmpD = T(), T()
            for ti in range(NT):
                ln_stats(ti, 5, xwork[:, ti, :], True, tmpD, T_tmpD)
                sw = statw[:, ti, :]
                tsk("sp", lambda s, r=tok0 + ti * 128: s.dma_start(
                    out=x0c2, in_=x_in[r:r + 128, :]),
                    writes=[T_x0c2], dma=True, dkey="x0b")
                tsk("act", lambda a, ti=ti, sw=sw: a.activation(
                    out=tmpD, in_=xwork[:, ti, :], func=AF.Identity,
                    scale=sw[:, 8:9], bias=sw[:, 9:10]),
                    reads=[T_xwork[ti], T_statw], writes=[T_tmpD])
                def f_x1(v, ti=ti):
                    return v.scalar_tensor_tensor(
                        out=xwork[:, ti, :], in0=tmpD, scalar=1.0, in1=x0c2,
                        op0=ALU.mult, op1=ALU.add,
                        accum_out=statw[:, ti, 10:11])
                tsk("dve", f_x1, reads=[T_tmpD, T_x0c2],
                    writes=[T_xwork[ti], T_statw])
                dbg(f"x1_{slab}_{ti}", xwork[:, ti, :], [T_xwork[ti]], [128, 1024])

            if stage_upto < 4:
                continue

            scr.reset()
            T_xdfm, T_hbuf, T_h2buf, T_ssb, T_tsb, T_ffm = T(), T(), T(), T(), T(), T()
            T_we = [T(), T()]
            T_we2 = [T(), T()]
            T_tmpD2, T_xdtm = T(), T()

            for ti in range(NT):
                ln_stats(ti, 10, xwork[:, ti, :], True, tmpD, T_tmpD2)
                sw = statw[:, ti, :]
                tsk("act", lambda a, ti=ti, sw=sw: a.activation(
                    out=xdtm, in_=xwork[:, ti, :], func=AF.Identity,
                    scale=sw[:, 13:14], bias=sw[:, 14:15]),
                    reads=[T_xwork[ti], T_statw], writes=[T_xdtm])

                def f_xdtr(t):
                    pa = psA.bitcast(BF16)
                    last = None
                    for ds in range(8):
                        last = t.transpose(pa[:, ds * 128:(ds + 1) * 128],
                                           xdtm[:, ds * 128:(ds + 1) * 128],
                                           id_bf16[:])
                    return fence(t)
                tsk("pe", f_xdtr, reads=[T_xdtm, T_id], writes=[T_psA])

                def f_xdev(a, ti=ti):
                    pa = psA.bitcast(BF16)
                    last = None
                    for ds in range(8):
                        last = a.activation(
                            out=xd_fm[:, ds, ti * 128:(ti + 1) * 128],
                            in_=pa[:, ds * 128:(ds + 1) * 128], func=AF.Copy)
                    return last
                tsk("act", f_xdev, reads=[T_psA], writes=[T_xdfm])

            for og in range(16):
                wbuf = wb_e[og % 2]
                tw1, tw2 = T_we[og % 2], T_we2[og % 2]
                w1v = wbuf[:, 0:2048].rearrange("p (k o) -> p k o", k=8)
                w2v = wbuf[:, 2048:4096].rearrange("p (k o) -> p k o", k=8)
                tsk("sp", lambda g, og=og, w1v=w1v: g.dma_start(
                    out=w1v, in_=w1_t[og].rearrange("s p t -> p s t")),
                    writes=[tw1], dma=True, dkey=f"te1_{og % 2}")
                tsk("sp", lambda g, og=og, w2v=w2v: g.dma_start(
                    out=w2v, in_=w2_t[og].rearrange("s p t -> p s t")),
                    writes=[tw2], dma=True, dkey=f"te2_{og % 2}")

                def f_l1(t, w1v=w1v, w2v=w2v):
                    last = None
                    for half in range(2):
                        wv = (w1v, w2v)[half]
                        for ot in range(2):
                            for k in range(8):
                                last = t.matmul(
                                    psA[:, (half * 2 + ot) * 512:
                                        (half * 2 + ot) * 512 + 512],
                                    wv[:, k, ot * 128:(ot + 1) * 128],
                                    xd_fm[:, k, :], start=(k == 0), stop=(k == 7))
                    return fence(t)
                tsk("pe", f_l1, reads=[tw1, tw2, T_xdfm], writes=[T_psA])

                def f_l1ev(a, og=og):
                    last = None
                    for ot in range(2):
                        last = a.activation(
                            out=s_sb[:, ot, :], in_=psA[:, ot * 512:ot * 512 + 512],
                            func=AF.Silu,
                            scale=sm["w1_s"][:, og * 2 + ot:og * 2 + ot + 1])
                    for ot in range(2):
                        last = a.activation(
                            out=t_sb[:, ot, :],
                            in_=psA[:, 1024 + ot * 512:1024 + ot * 512 + 512],
                            func=AF.Identity,
                            scale=sm["w2_s"][:, og * 2 + ot:og * 2 + ot + 1])
                    return last
                tsk("act", f_l1ev, reads=[T_psA, T_sm["w1_s"], T_sm["w2_s"]],
                    writes=[T_ssb, T_tsb])

                tsk("dve", lambda v, og=og: v.tensor_mul(
                    hbuf[:, og * 2:og * 2 + 2, :].rearrange("p a t -> p (a t)"),
                    s_sb[:].rearrange("p a t -> p (a t)"),
                    t_sb[:].rearrange("p a t -> p (a t)")),
                    reads=[T_ssb, T_tsb], writes=[T_hbuf])

            def tern_layer(wdram, scale_sm, src_buf, T_src, dst_buf, T_dst, dkpre):
                for og in range(16):
                    wbuf, tw = wb_e[og % 2], T_we[og % 2]
                    wv = wbuf[:, 0:8192].rearrange("p (k o) -> p k o", k=32)
                    tsk("sp", lambda g, og=og, wv=wv, wdram=wdram: g.dma_start(
                        out=wv, in_=wdram[og].rearrange("s p t -> p s t")),
                        writes=[tw], dma=True, dkey=f"{dkpre}_{og % 2}")

                    def f_mm(t, wv=wv):
                        last = None
                        for ot in range(2):
                            for k in range(32):
                                last = t.matmul(
                                    psA[:, ot * 512:ot * 512 + 512],
                                    wv[:, k, ot * 128:(ot + 1) * 128],
                                    src_buf[:, k, :], start=(k == 0),
                                    stop=(k == 31))
                        return fence(t)
                    tsk("pe", f_mm, reads=[tw, T_src], writes=[T_psA])

                    def f_ev(a, og=og):
                        last = None
                        for ot in range(2):
                            last = a.activation(
                                out=dst_buf[:, og * 2 + ot, :],
                                in_=psA[:, ot * 512:ot * 512 + 512], func=AF.Gelu,
                                scale=scale_sm[:, og * 2 + ot:og * 2 + ot + 1])
                        return last
                    tsk("act", f_ev, reads=[T_psA], writes=[T_dst])

            tern_layer(h1_t, sm["h1_s"], hbuf, T_hbuf, h2buf, T_h2buf, "th1")
            tern_layer(h2_t, sm["h2_s"], h2buf, T_h2buf, hbuf, T_hbuf, "th2")

            for og in range(4):
                wbuf, tw = wb_e[og % 2], T_we[og % 2]
                wv = wbuf[:, 0:8192].rearrange("p (k o) -> p k o", k=32)
                tsk("sp", lambda g, og=og, wv=wv: g.dma_start(
                    out=wv, in_=w3_t[og].rearrange("s p t -> p s t")),
                    writes=[tw], dma=True, dkey=f"tw3_{og % 2}")

                def f_w3(t, wv=wv):
                    last = None
                    for ot in range(2):
                        for k in range(32):
                            last = t.matmul(psA[:, ot * 512:ot * 512 + 512],
                                            wv[:, k, ot * 128:(ot + 1) * 128],
                                            hbuf[:, k, :], start=(k == 0),
                                            stop=(k == 31))
                    return fence(t)
                tsk("pe", f_w3, reads=[tw, T_hbuf], writes=[T_psA])

                def f_w3ev(a, og=og):
                    last = None
                    for ot in range(2):
                        last = a.activation(
                            out=f_fm[:, og * 2 + ot, :],
                            in_=psA[:, ot * 512:ot * 512 + 512], func=AF.Identity,
                            scale=sm["w3_s"][:, og * 2 + ot:og * 2 + ot + 1])
                    return last
                tsk("act", f_w3ev, reads=[T_psA, T_sm["w3_s"]], writes=[T_ffm],
                    extra=T_xdfm.wr() + T_ssb.wr() + T_tsb.wr() + T_xdtm.wr())

            if stage_upto < 5:
                continue

            scr.reset()
            T_ftmp, T_tmpD3 = T(), T()
            x2_evs = []
            for ti in range(NT):
                def f_ftr(t, ti=ti):
                    last = None
                    for ds in range(8):
                        last = t.transpose(psB[:, ds * 128:(ds + 1) * 128],
                                           f_fm[:, ds, ti * 128:(ti + 1) * 128],
                                           id_f32[:])
                    return fence(t)
                tsk("pe", f_ftr, reads=[T_ffm, T_id], writes=[T_psB])

                def f_fev(v, ti=ti):
                    return v.tensor_scalar(
                        out=ftmp, in0=psB[:, 0:1024], scalar1=1.0, scalar2=0.0,
                        op0=ALU.mult, op1=ALU.add, accum_out=statw[:, ti, 15:16])
                tsk("dve", f_fev, reads=[T_psB], writes=[T_ftmp, T_statw])

                ln_stats(ti, 15, ftmp, True, tmpD, T_tmpD3)
                sw = statw[:, ti, :]
                tsk("act", lambda a, ti=ti, sw=sw: a.activation(
                    out=tmpD, in_=ftmp, func=AF.Identity,
                    scale=sw[:, 18:19], bias=sw[:, 19:20]),
                    reads=[T_ftmp, T_statw], writes=[T_tmpD])

                def f_x2(v, ti=ti):
                    return v.scalar_tensor_tensor(
                        out=xwork[:, ti, :], in0=tmpD, scalar=1.0,
                        in1=xwork[:, ti, :], op0=ALU.mult, op1=ALU.add)
                ev = tsk("dve", f_x2, reads=[T_tmpD], writes=[T_xwork[ti]])
                x2_evs.append(ev)
                dbg(f"x2_{slab}_{ti}", xwork[:, ti, :], [T_xwork[ti]], [128, 1024])

            for ti in range(NT):
                ev = tsk("sp", lambda s, ti=ti, r=tok0 + ti * 128: s.dma_start(
                    out=x_out[r:r + 128, :], in_=xwork[:, ti, :]),
                    reads=[T_xwork[ti]], extra=x2_evs,
                    dma=True, dkey=f"xo{ti}")
                out_evs.append(ev)

        p.waits("sp", out_evs)

        p.fences = {}
        sems = {s: e(nc.semaphore(s)) for s in
                list(MSEM.values()) + p.dma_sems}
        mk = p.replayer(sems)
        with nc.Block() as block:
            block.sync(lambda eng: mk("sp", eng))
            block.vector(lambda eng: mk("dve", eng))
            block.scalar(lambda eng: mk("act", eng))
            block.gpsimd(lambda eng: mk("pool", eng))
            block.tensor(lambda eng: mk("pe", eng))

    return nc


def make_in_maps(inp):
    tiles = prep_inputs(inp)
    x = np.ascontiguousarray(np.asarray(inp["L0_attn_x"], dtype=np.float32))
    in_maps = []
    for core in range(NCORE):
        m = dict(tiles)
        m["x"] = np.ascontiguousarray(
            x[core * NSEQ:(core + 1) * NSEQ].reshape(NSEQ * L, D))
        in_maps.append(m)
    return in_maps


def kernel(**inp):
    nc = build_program()
    in_maps = make_in_maps(inp)
    res = run_bass_kernel_spmd(nc, in_maps, core_ids=list(range(NCORE)))
    out = np.concatenate([r["out"].reshape(NSEQ, L, D) for r in res.results], axis=0)
    return out.astype(np.float32)
